# version 11 (final)
# baseline (speedup 1.0000x reference)
import numpy as np
import concourse.bass as bass
import concourse.mybir as mybir
from concourse.ap import AP
from concourse.bass import IndirectOffsetOnAxis

F32 = mybir.dt.float32
BF16 = mybir.dt.bfloat16
U32 = mybir.dt.uint32
AF = mybir.ActivationFunctionType
ALU = mybir.AluOpType
AX = mybir.AxisListType

ENGS = ("sync", "tensor", "vector", "scalar", "gpsimd")
SY, T, V, SC, G = ENGS


class Cfg:
    def __init__(self, S=2048, DM=2048, NB=2, DI=1408, ISH=2816, CAP=1280,
                 W=1024, E=8):
        self.S, self.DM, self.NB = S, DM, NB
        self.H = DM // 64
        self.E, self.DI, self.ISH, self.CAP, self.W = E, DI, ISH, CAP, W
        self.NT = S // 128
        self.NK = DM // 128
        self.NTOK = NB * S
        self.NF = self.NTOK // 128
        self.NIG = S // 512
        self.C_EB = 896 + min(W, max(0, S - 512))
        self.PADN = 512
        self.NI = DI // 128
        self.NIS = ISH // 128
        self.NCAP = CAP // 128
        assert S % 512 == 0 and DM % 128 == 0 and DI % 128 == 0
        assert ISH % 128 == 0 and CAP % 128 == 0 and W % 128 == 0
        assert self.NF <= 128 and self.E == 8 and self.NK % 4 == 0


class Sem:
    __slots__ = ("h", "n", "name")

    def __init__(self, h, name):
        self.h, self.n, self.name = h, 0, name

    def snap(self):
        return (self, self.n)


class Prog:
    def __init__(self, nc, ctx):
        self.nc, self.ctx = nc, ctx
        self.stream = {e: [] for e in ENGS}

    def sem(self, name):
        return Sem(self.ctx.enter_context(self.nc.semaphore(name)), name)

    def emit(self, eng, fn, waits=(), incs=()):
        ws = [(s, int(v)) for (s, v) in waits if s is not None]
        ics = []
        for (s, k) in incs:
            s.n += k
            ics.append((s, k))
        self.stream[eng].append((fn, ws, ics))

    def dma(self, q, out, in_, sem, waits=(), **kw):
        self.emit(q, lambda e, o=out, i=in_, kw=kw: e.dma_start(o, i, **kw),
                  waits, [(sem, 16)])

    def barrier(self):
        return [s.snap() for s in (self.s_ten, self.s_vec, self.s_scl,
                                   self.s_gpe)]

    def finalize(self, block):
        for eng in ENGS:
            ops = self.stream[eng]
            if not ops:
                continue

            def body(e, ops=ops):
                for fn, ws, ics in ops:
                    for s, v in ws:
                        e.wait_ge(s.h, v)
                    inst = fn(e)
                    if inst is None:
                        continue
                    for s, k in ics:
                        inst.then_inc(s.h, k)
            getattr(block, eng)(body)


class Stream:

    def __init__(self, P, name, nslots=2):
        self.P, self.nslots = P, nslots
        self.ld = [P.sem(f"{name}_l{i}") for i in range(nslots)]
        self.k = 0
        self.war = [[] for _ in range(nslots)]

    def load(self, q, out, in_, waits=(), **kw):
        i = self.k % self.nslots
        ws = list(waits) + list(self.war[i])
        self.P.dma(q, out, in_, self.ld[i], waits=ws, **kw)
        self.k += 1
        return (self.ld[i], 16 * ((self.k - 1) // self.nslots + 1))

    def emit_load(self, eng, fn, waits=()):
        i = self.k % self.nslots
        ws = list(waits) + list(self.war[i])
        self.P.emit(eng, fn, ws, [(self.ld[i], 16)])
        self.k += 1
        return (self.ld[i], 16 * ((self.k - 1) // self.nslots + 1))

    def set_war(self, snap):
        w = snap if isinstance(snap, list) else [snap]
        self.war[(self.k - 1) % self.nslots] = list(w)

    def total(self):
        return [(self.ld[i], 16 * ((self.k - 1 - i) // self.nslots + 1))
                for i in range(min(self.k, self.nslots))]


def cdiv(a, b):
    return (a + b - 1) // b


def build_program(c: Cfg, nc, ctx):
    P = Prog(nc, ctx)
    dt = nc.dram_tensor
    AR1W, AR2W, AR3W = 45184, c.NK * c.S, 16384

    x_d = dt("x", [c.NB, c.S, c.DM], F32, kind="ExternalInput").ap()
    wqT_d = dt("wqT", [c.DM, c.DM], F32, kind="ExternalInput").ap()
    wkT_d = dt("wkT", [c.DM, c.DM], F32, kind="ExternalInput").ap()
    wvT_d = dt("wvT", [c.DM, c.DM], F32, kind="ExternalInput").ap()
    woT_d = dt("woT", [c.DM, c.DM], F32, kind="ExternalInput").ap()
    lb_d = dt("lb", [c.H, c.W], F32, kind="ExternalInput").ap()
    soff_d = dt("soff", [1, c.H], F32, kind="ExternalInput").ap()
    gwT_d = dt("gwT", [c.DM, c.E], F32, kind="ExternalInput").ap()
    wg_d = dt("wg", [c.E, c.DM, c.DI], F32, kind="ExternalInput").ap()
    wu_d = dt("wu", [c.E, c.DM, c.DI], F32, kind="ExternalInput").ap()
    wd_d = dt("wd", [c.E, c.DI, c.DM], F32, kind="ExternalInput").ap()
    swg_d = dt("swg", [c.DM, c.ISH], F32, kind="ExternalInput").ap()
    swu_d = dt("swu", [c.DM, c.ISH], F32, kind="ExternalInput").ap()
    swd_d = dt("swd", [c.ISH, c.DM], F32, kind="ExternalInput").ap()
    cst_d = dt("cst", [128, 1], F32, kind="ExternalInput").ap()
    identb_d = dt("identb", [128, 128], BF16, kind="ExternalInput").ap()
    identf_d = dt("identf", [128, 128], F32, kind="ExternalInput").ap()
    ropeC_d = dt("ropeC", [128, c.S], F32, kind="ExternalInput").ap()
    ropeS_d = dt("ropeS", [128, c.S], F32, kind="ExternalInput").ap()
    triu_d = dt("triu", [128, 128], F32, kind="ExternalInput").ap()
    onesr_d = dt("onesr", [1, 128], F32, kind="ExternalInput").ap()
    onesc_d = dt("onesc", [128, 1], F32, kind="ExternalInput").ap()
    dgidx_d = dt("dgidx", [128, 1], U32, kind="ExternalInput").ap()
    out_d = dt("out", [c.NB, c.S, c.DM], F32, kind="ExternalOutput").ap()

    z1_d = dt("z1s", [c.NB, c.S, c.DM], F32, kind="Internal").ap()
    attnT_d = dt("attnTs", [c.NB, c.NT, 128, c.NK * 128], BF16,
                 kind="Internal").ap()
    explb_d = dt("explbs", [c.H, c.PADN + c.S], F32, kind="Internal").ap()
    ln2t_d = dt("ln2ts", [c.NTOK, c.DM], BF16, kind="Internal").ap()
    preout_d = dt("preouts", [c.NTOK, c.DM], F32, kind="Internal").ap()
    expin_d = dt("expins", [c.E * c.CAP, c.DM], BF16, kind="Internal").ap()
    eo_d = dt("eos", [c.E * c.CAP, c.DM], BF16, kind="Internal").ap()

    sbt = lambda n, sh, d: ctx.enter_context(nc.sbuf_tensor(n, sh, d))
    AR1 = sbt("AR1", [128, AR1W], BF16)
    AR2 = sbt("AR2", [128, AR2W], BF16)
    AR3 = sbt("AR3", [128, AR3W], BF16)

    def a1(col, n):
        return AR1[:, col:col + n]

    def a2(col, n):
        return AR2[:, col:col + n]

    def a3(col, n):
        return AR3[:, col:col + n]

    def a1f(col, n):
        return AR1[:, 2 * col:2 * (col + n)].bitcast(F32)

    def a2f(col, n):
        return AR2[:, 2 * col:2 * (col + n)].bitcast(F32)

    def a3f(col, n):
        return AR3[:, 2 * col:2 * (col + n)].bitcast(F32)

    cstv = sbt("cstvs", [128, 1], F32)
    identb = sbt("identbs", [128, 128], BF16)
    identf = sbt("identfs", [128, 128], F32)
    triu = sbt("trius", [128, 128], F32)
    onesr = sbt("onesrs", [1, 128], F32)
    onesc = sbt("onescs", [128, 1], F32)
    dgidx = sbt("dgidxs", [128, 1], U32)
    gwT = sbt("gwTs", [128, c.NK * c.E], BF16)
    soff64 = sbt("soff64s", [128, 2 * c.H], F32)
    ropeC = sbt("ropeCs", [128, c.S], BF16)
    ropeS = sbt("ropeSs", [128, c.S], BF16)
    ltok = sbt("ltoks", [128, c.NF * c.E], F32)
    gmax = sbt("gmaxs", [128, c.NF], F32)
    gsum = sbt("gsums", [128, c.NF], F32)
    probs = sbt("probss", [128, c.NF * c.E], F32)
    top8 = sbt("top8s", [128, 8], F32)
    idx8 = sbt("idx8s", [128, 8], U32)
    i12f = sbt("i12fs", [128, 2 * c.NF], F32)
    w12 = sbt("w12s", [128, 2 * c.NF], F32)
    pos12 = sbt("pos12s", [128, 2 * c.NF], F32)
    scatf = sbt("scatfs", [128, 2 * c.NF], F32)
    scat = sbt("scats", [128, 2 * c.NF], U32)
    mske = sbt("mskes", [128, c.NF], F32)
    msk1 = sbt("msk1s", [128, c.NF], F32)
    pose = sbt("poses", [128, c.NF], F32)
    csum = sbt("csums", [1, c.NF], F32)
    csumT = sbt("csumTs", [128, 1], F32)
    carrys = sbt("carryss", [1, c.NF], F32)
    stats = sbt("statss", [128, 8], F32)
    statb = sbt("statbs", [128, 8], F32)

    PS1 = ctx.enter_context(nc.psum_tensor("PS1", [128, 2048], F32))
    PS2 = ctx.enter_context(nc.psum_tensor("PS2", [128, 2048], F32))
    PS2W = 2048
    trps = PS2[:, 1536:2048].bitcast(BF16)

    s_vec = P.sem("vec")
    s_scl = P.sem("scl")
    s_ten = P.sem("ten")
    s_gpe = P.sem("gpe")
    s_sydma = P.sem("sydma")
    s_gpdma = P.sem("gpdma")
    s_exl = P.sem("exl")
    s_zr = P.sem("zr")
    s_woL = P.sem("woL")
    s_swdL = P.sem("swdL")
    s_wdL = P.sem("wdL")
    s_boot = P.sem("boot")
    P.s_vec, P.s_scl, P.s_ten, P.s_gpe = s_vec, s_scl, s_ten, s_gpe

    st_x = Stream(P, "x")
    st_w = Stream(P, "w")
    st_w2 = Stream(P, "w2")
    st_wv = Stream(P, "wv", 1)
    st_at = Stream(P, "at", 2)
    st_swd = Stream(P, "swd", 8)
    st_eb = Stream(P, "eb")
    st_ln = Stream(P, "ln")
    st_ga = Stream(P, "ga")
    st_gb = Stream(P, "gb")
    st_xg = Stream(P, "xg")
    st_z = Stream(P, "z")
    st_lnst = Stream(P, "lnst")
    st_po = Stream(P, "po", 1)
    st_eo = Stream(P, "eo")
    st_sc = Stream(P, "sc")
    st_out = Stream(P, "outs")

    nc.const_aps.register(F32, 1e-5, cstv[:, 0:1])
    for dst, src in ((cstv[:], cst_d), (identb[:], identb_d),
                     (identf[:], identf_d),
                     (triu[:], triu_d), (onesr[:], onesr_d),
                     (onesc[:], onesc_d), (dgidx[:], dgidx_d),
                     (ropeC[:], ropeC_d),
                     (ropeS[:], ropeS_d), (soff64[64:65, 0:c.H], soff_d)):
        P.dma(G, dst, src, s_boot)
    P.dma(G, gwT[:].rearrange("p (k e) -> p k e", e=c.E),
          AP(gwT_d.tensor, 0, [[c.E, 128], [128 * c.E, c.NK], [1, c.E]]),
          s_boot)
    boot_w = s_boot.snap()

    LEX = c.PADN + c.S
    exl = a1f(2048, LEX)[0:c.H, :]
    exl2 = a1f(2048 + LEX, LEX)[0:c.H, :]
    P.emit(G, lambda e: e.memset(exl[:, 0:c.PADN], -1e30), [], [(s_gpe, 1)])
    P.emit(G, lambda e: e.memset(exl[:, c.PADN + c.W:LEX], 0.0), [],
           [(s_gpe, 1)])
    P.dma(G, exl[:, c.PADN:c.PADN + c.W], lb_d, s_exl, waits=[s_gpe.snap()])
    P.emit(SC, lambda e: e.activation(exl2, exl, AF.Exp),
           [s_exl.snap(), s_gpe.snap()], [(s_scl, 1)])
    P.dma(G, explb_d, exl2, s_exl, waits=[s_scl.snap()])
    explb_w = s_exl.snap()

    P.emit(SC, lambda e: e.activation(soff64[64:65, c.H:2 * c.H],
                                      soff64[64:65, 0:c.H], AF.Exp),
           [boot_w], [(s_scl, 1)])
    eoff_w = s_scl.snap()

    ZW = 4096
    zt = a1(0, ZW)
    P.emit(G, lambda e: e.memset(zt, 0.0), [], [(s_gpe, 1)])
    ztot = c.E * c.CAP * c.DM
    assert ztot % ZW == 0
    zoff = 0
    while zoff < ztot:
        n = min(128 * ZW, ztot - zoff)
        npart = n // ZW
        P.dma(G, AP(expin_d.tensor, zoff, [[ZW, npart], [1, ZW]]),
              zt[0:npart, :], s_zr, waits=[s_gpe.snap()])
        zoff += n
    expin_zero_w = s_zr.snap()

    def transpose_tile(src_sb_fn, dst_t, dst_col0, dst_stride, t0, nk,
                       first_waits):
        for g0 in range(0, nk, 8):
            gn = min(8, nk - g0)
            for j in range(gn):
                k = g0 + j
                ws = [s_vec.snap()] if j == 0 else []
                if g0 == 0 and j == 0:
                    ws += list(first_waits)
                P.emit(T, lambda e, k=k, j=j: e.transpose(
                    trps[:, j * 128:(j + 1) * 128], src_sb_fn(k), identb[:]),
                    ws, [(s_ten, 1)] if j == gn - 1 else [])

            def ev(e, g0=g0, gn=gn):
                dst = AP(dst_t, dst_col0 + g0 * dst_stride + t0,
                         [[int(dst_t.shape[1]), 128],
                          [dst_stride, gn], [1, 128]])
                src = trps[:, 0:gn * 128].rearrange("p (g f) -> p g f", g=gn)
                return e.tensor_copy(dst, src)
            P.emit(V, ev, [s_ten.snap()], [(s_vec, 1)])
        return s_vec.snap()

    def ln_phase(b, kind, entry):
        evw = None
        lnst_w = {0: [], 1: []}
        src = x_d[b] if kind == "ln1" else z1_d[b]
        lane_w = {0: [], 1: []}
        sq_w = {0: [], 1: []}
        for t in range(c.NT):
            sl = t % 2
            S0, S1, S2, S3 = 4 * sl, 4 * sl + 1, 4 * sl + 2, 4 * sl + 3
            B0, B1, B2 = 2 * sl, 2 * sl + 1, 4 + sl
            xt = a3f(3 * c.DM // 2 + sl * c.DM, c.DM)
            lnt = a3(sl * c.DM, c.DM)
            xw = st_x.load(SY, xt, src[t * 128:(t + 1) * 128, :],
                           waits=list(entry) if t < 2 else [])
            P.emit(V, lambda e, xt=xt, S0=S0: e.reduce_sum(
                stats[:, S0:S0 + 1], xt, axis=AX.X),
                [xw] + lane_w[sl] + sq_w[sl], [(s_vec, 1)])
            sum_w = s_vec.snap()
            P.emit(SC, lambda e, xt=xt, S1=S1: e.activation(
                a3(2 * c.DM, c.DM), xt,
                AF.Square, accum_out=stats[:, S1:S1 + 1]),
                [xw] + lane_w[sl] + sq_w[0] + sq_w[1], [(s_scl, 1)])
            sq_done = s_scl.snap()
            P.emit(V, lambda e, S0=S0, S2=S2: e.tensor_mul(
                stats[:, S2:S2 + 1], stats[:, S0:S0 + 1],
                stats[:, S0:S0 + 1]), [sum_w, s_vec.snap()], [(s_vec, 1)])
            P.emit(V, lambda e, S1=S1, S2=S2, S3=S3: e.scalar_tensor_tensor(
                stats[:, S3:S3 + 1], stats[:, S2:S2 + 1], -1.0 / c.DM,
                stats[:, S1:S1 + 1], ALU.mult, ALU.add),
                [s_vec.snap(), sq_done], [(s_vec, 1)])
            P.emit(SC, lambda e, S3=S3, B0=B0: e.activation(
                statb[:, B0:B0 + 1], stats[:, S3:S3 + 1], AF.Sqrt, bias=1e-5,
                scale=1.0 / c.DM), [s_vec.snap(), boot_w], [(s_scl, 1)])
            P.emit(V, lambda e, B0=B0, B1=B1: e.reciprocal(
                statb[:, B1:B1 + 1], statb[:, B0:B0 + 1]),
                [s_scl.snap()], [(s_vec, 1)])
            P.emit(V, lambda e, S0=S0, B2=B2: e.tensor_scalar(
                statb[:, B2:B2 + 1], stats[:, S0:S0 + 1], -1.0 / c.DM, None,
                ALU.mult), [sum_w, s_vec.snap()], [(s_vec, 1)])
            P.emit(V, lambda e, xt=xt, lnt=lnt, B1=B1, B2=B2: e.tensor_scalar(
                lnt, xt, statb[:, B2:B2 + 1], statb[:, B1:B1 + 1],
                ALU.add, ALU.mult),
                [s_vec.snap()] + lnst_w[sl] + (list(entry) if t < 2 else []),
                [(s_vec, 1)])
            lane_w[sl] = [s_vec.snap()]
            sq_w[sl] = [s_scl.snap()]
            st_x.set_war(s_vec.snap())
            ln_w = s_vec.snap()
            if kind == "ln2":
                ng = b * c.NT + t
                lnst_w[sl] = [st_lnst.load(
                    SC, ln2t_d[ng * 128:(ng + 1) * 128, :], lnt,
                    waits=[ln_w])]
            evw = transpose_tile(
                lambda k, lnt=lnt: lnt[:, k * 128:(k + 1) * 128],
                AR2, 0, c.S, t * 128, c.NK,
                [ln_w] + (list(entry) if t == 0 else []))
            if kind == "ln2":
                pass
            if kind == "ln2":
                ng = b * c.NT + t
                lps = PS1[0:c.E, 256:384]
                for k in range(c.NK):
                    P.emit(T, lambda e, k=k, t=t: e.matmul(
                        lps[:, 0:128], gwT[:, k * c.E:(k + 1) * c.E],
                        a2(k * c.S + t * 128, 128),
                        start=(k == 0), stop=(k == c.NK - 1),
                        skip_group_check=True),
                        [evw, boot_w, s_vec.snap()] if k == 0 else [],
                        [(s_ten, 1)] if k == c.NK - 1 else [])
                lsb = a3f(7 * c.DM // 2, 128)[0:c.E, :]
                P.emit(V, lambda e, lsb=lsb: e.tensor_copy(lsb, lps[:, 0:128]),
                       [s_ten.snap()], [(s_vec, 1)])
                tps = PS2[:, 1280:1280 + c.E]
                P.emit(T, lambda e, lsb=lsb, tps=tps: e.transpose(
                    tps, lsb, identf[0:c.E, 0:c.E]),
                    [s_vec.snap()], [(s_ten, 1)])
                P.emit(V, lambda e, ng=ng, tps=tps: e.tensor_copy(
                    ltok[:, ng * c.E:(ng + 1) * c.E], tps),
                    [s_ten.snap()], [(s_vec, 1)])
        return s_vec.snap()

    PB_COL = 0
    NPSL = 6
    VB_COL = NPSL * 2048
    KB_COL = VB_COL + c.NT * 260
    QB_COL = KB_COL + c.S
    QSB_COL = QB_COL + c.S
    QSW_COL = QSB_COL + c.S
    DNM_F = (QSW_COL + c.S) // 2
    WV_COL = QSW_COL + c.S + 8192
    assert WV_COL + c.NK * 256 <= AR1W
    EB_COL = 4096
    EXPS_COL = EB_COL + 2 * c.C_EB
    STG_COL = EXPS_COL + 4 * 512
    RSCR_F = (STG_COL + 2 * c.DM) // 2
    assert 2 * RSCR_F + 2048 <= AR3W
    BAND = c.W + 128
    WQK_COL = 0

    def attn_batch(b, lnxT_w, entry):
        dr = a1f(DNM_F, 4096)

        def dnm(s):
            return dr[64:65, s * 2048:(s + 1) * 2048]

        def rec(s, col0=0, n=2048):
            return dr[0:1, s * 2048 + col0:s * 2048 + col0 + n]

        def rtile(s):
            return dr[64:128, s * 2048:s * 2048 + 32]

        ent = list(entry)
        slot_war = [ent + [s_vec.snap()] for _ in range(4)]
        exps_war = [ent + [s_vec.snap()] for _ in range(4)]
        pslot_war = [ent + [s_ten.snap()] for _ in range(NPSL)]
        bp_war = {q: ent + [s_vec.snap()] for q in range(4)}
        stage_war = {0: list(ent), 1: list(ent)}
        dnm_war = {0: [], 1: []}
        qsb_war = [[s_vec.snap()]]
        qsw_war = [[s_vec.snap()]]
        piece_ready = {}
        bank_done = {}
        evac_done = {}
        mult_done = {}
        recip_done = {}
        vready = [None]
        qk_w = [None]
        ps1_i = [0]
        exps_i = [0]
        woL_ws = []

        def ps1_slot():
            i = ps1_i[0] % 4
            ps1_i[0] += 1
            return i

        def vproj(sb):
            wv = a1(WV_COL, c.NK * 256)
            wvw = st_wv.load(G, wv.rearrange("p (k f) -> p k f", f=256),
                             AP(wvT_d.tensor, sb * 256,
                                [[c.DM, 128], [128 * c.DM, c.NK], [1, 256]]),
                             waits=(ent + [lnxT_w]) if sb == 0 else [])
            for t in range(c.NT):
                si = ps1_slot()
                sps = PS1[:, si * 512:si * 512 + 256]
                for k in range(c.NK):
                    P.emit(T, lambda e, k=k, t=t, sps=sps, wv=wv: e.matmul(
                        sps, a2(k * c.S + t * 128, 128),
                        wv[:, k * 256:(k + 1) * 256],
                        start=(k == 0), stop=(k == c.NK - 1),
                        skip_group_check=True),
                        ([wvw] + slot_war[si] +
                         ([lnxT_w] if t == 0 else [])) if k == 0 else [],
                        [(s_ten, 1)] if k == c.NK - 1 else [])
                vdone = s_ten.snap()

                def vev(e, t=t, sps=sps):
                    dst = AP(AR1, VB_COL + t * 260,
                             [[AR1W, 128], [65, 4], [1, 64]])
                    return e.tensor_copy(dst, sps.rearrange(
                        "p (h f) -> p h f", h=4))
                P.emit(V, vev, [vdone] + (ent if (sb == 0 and t == 0)
                                          else []), [(s_vec, 1)])
                slot_war[si] = [s_vec.snap()]
            st_wv.set_war([s_ten.snap()])

            def vones(e):
                dst = AP(AR1, VB_COL + 64,
                         [[AR1W, 128], [260, c.NT], [65, 4], [1, 1]])
                return e.memset(dst, 1.0)
            P.emit(V, vones, [s_vec.snap()], [(s_vec, 1)])
            vready[0] = s_vec.snap()

        def qkproj(p):
            for which, wT_d, dcol, kscale in (
                    ("q", wqT_d, QB_COL, 1.0), ("k", wkT_d, KB_COL, 0.125)):
                wslot = st_w2.k % 2
                wqk = a3(WQK_COL + wslot * 2048, 2048)
                ww = st_w2.load(G, wqk.rearrange("p (k f) -> p k f", f=128),
                                AP(wT_d.tensor, p * 128,
                                   [[c.DM, 128], [128 * c.DM, c.NK],
                                    [1, 128]]),
                                waits=(ent + [lnxT_w])
                                if (p == 0 and which == "q") else [])
                qsb = a1(QSB_COL, c.S)
                qsw = a1(QSW_COL, c.S)
                for ns in range(4):
                    si = ps1_slot()
                    sps = PS1[:, si * 512:(si + 1) * 512]
                    for k in range(c.NK):
                        P.emit(T, lambda e, k=k, ns=ns, sps=sps, wqk=wqk:
                               e.matmul(
                            sps, wqk[:, k * 128:(k + 1) * 128],
                            a2(k * c.S + ns * 512, 512),
                            start=(k == 0), stop=(k == c.NK - 1),
                            skip_group_check=True),
                            ([ww] + slot_war[si] +
                             ([lnxT_w] if (p == 0 and ns == 0) else []))
                            if k == 0 else [],
                            [(s_ten, 1)] if k == c.NK - 1 else [])
                    pdone = s_ten.snap()
                    P.emit(V, lambda e, ns=ns, sps=sps, qsb=qsb, ks=kscale:
                           e.tensor_scalar(
                        qsb[:, ns * 512:(ns + 1) * 512], sps, ks, None,
                        ALU.mult),
                        [pdone] + (qsb_war[0] if ns == 0 else []),
                        [(s_vec, 1)])
                    slot_war[si] = [s_vec.snap()]
                st_w2.set_war([s_ten.snap()])
                qsb_w = s_vec.snap()
                for (d0, sc0) in ((0, 32), (32, 0), (64, 96), (96, 64)):
                    P.dma(SY, qsw[d0:d0 + 32, :], qsb[sc0:sc0 + 32, :],
                          s_sydma, waits=[qsb_w] + qsw_war[0])
                dstb = a1(dcol, c.S)
                P.emit(V, lambda e, dstb=dstb, qsb=qsb: e.tensor_mul(
                    dstb, qsb, ropeC[:, 0:c.S]),
                    [boot_w, s_ten.snap(), s_vec.snap()], [(s_vec, 1)])
                P.emit(V, lambda e, qsb=qsb, qsw=qsw: e.tensor_mul(
                    qsb, qsw, ropeS[:, 0:c.S]),
                    [s_sydma.snap()], [(s_vec, 1)])
                qsw_war[0] = [s_vec.snap()]
                P.emit(V, lambda e, dstb=dstb, qsb=qsb: e.tensor_add(
                    dstb, dstb, qsb), [], [(s_vec, 1)])
                qsb_war[0] = [s_vec.snap()]
            qk_w[0] = s_vec.snap()

        def s_strip(h, J, eb_col, ebw, first_mm):
            hh2 = h % 2
            sl = (h * 16 + J) % NPSL
            pw = pslot_war[sl]
            fscl, fvec = [True], [True]
            for q in range(J // 4, 4):
                c0 = max(0, 512 * q - 128 * J)
                c1 = 512 * (q + 1) - 128 * J
                w = c1 - c0
                i0 = 128 * J + c0
                si = ps1_slot()
                sps = PS1[:, si * 512:si * 512 + w]
                P.emit(T, lambda e, J=J, i0=i0, w=w, sps=sps, hh2=hh2:
                       e.matmul(
                    sps, a1(KB_COL + J * 128, 128)[64 * hh2:64 * hh2 + 64, :],
                    a1(QB_COL + i0, w)[64 * hh2:64 * hh2 + 64, :],
                    start=True, stop=True, skip_group_check=True),
                    slot_war[si] + ([qk_w[0]] if first_mm[0] else []),
                    [(s_ten, 1)])
                first_mm[0] = False
                smm = s_ten.snap()
                rdy = []
                ib1 = min(c1, BAND)
                if ib1 > c0:
                    ei = exps_i[0] % 4
                    exps_i[0] += 1
                    es = a3(EXPS_COL + ei * 512, 512)
                    P.emit(SC, lambda e, es=es, sps=sps, n=ib1 - c0:
                           e.activation(es[:, 0:n], sps[:, 0:n], AF.Exp),
                           [smm] + exps_war[ei], [(s_scl, 1)])
                    sw = s_scl.snap()
                    P.emit(V, lambda e, sl=sl, c0=c0, n=ib1 - c0, es=es,
                           eb_col=eb_col: e.tensor_mul(
                        a1(PB_COL + sl * 2048 + c0, n), es[:, 0:n],
                        a3(eb_col + 384 + c0, n)),
                        [sw, ebw] + (pw if fvec[0] else []),
                        [(s_vec, 1)])
                    fvec[0] = False
                    exps_war[ei] = [s_vec.snap()]
                    rdy.append(s_vec.snap())
                if c1 > ib1:
                    o0 = max(c0, ib1)
                    P.emit(SC, lambda e, sl=sl, o0=o0, c0=c0, c1=c1, sps=sps:
                           e.activation(
                        a1(PB_COL + sl * 2048 + o0, c1 - o0),
                        sps[:, o0 - c0:c1 - c0], AF.Exp),
                        [smm] + (pw if fscl[0] else []),
                        [(s_scl, 1)])
                    fscl[0] = False
                    rdy.append(s_scl.snap())
                slot_war[si] = [s_scl.snap()]
                piece_ready[(h, J, q)] = rdy

        def pv_strip(h, J):
            hh4 = h % 4
            sl = (h * 16 + J) % NPSL
            for q in range(J // 4, 4):
                c0 = max(0, 512 * q - 128 * J)
                c1 = 512 * (q + 1) - 128 * J
                o0 = 128 * J + c0
                w = c1 - c0
                start = (J == 0)
                stop = (J == min(4 * q + 3, 15))
                ws = list(piece_ready.pop((h, J, q), []))
                if start:
                    ws += bp_war[q] + [vready[0]]
                P.emit(T, lambda e, hh4=hh4, sl=sl, c0=c0, o0=o0, w=w, J=J,
                       start=start, stop=stop: e.matmul(
                    PS2[0:65, o0:o0 + w],
                    AP(AR1, VB_COL + J * 260 + hh4 * 65,
                       [[AR1W, 128], [1, 65]]),
                    a1(PB_COL + sl * 2048 + c0, w),
                    start=start, stop=stop, skip_group_check=True),
                    ws, [(s_ten, 1)] if (stop or q == 3) else [])
                if stop:
                    bank_done[(h, q)] = s_ten.snap()
            pslot_war[sl] = [s_ten.snap()]

        def head_tail(h):
            ssl = h % 2
            stg = a3(STG_COL + ssl * 2048, 2048)[0:64, :]
            dws = []
            for g in (0, 1):
                bws = [bank_done[(h, 2 * g)], bank_done[(h, 2 * g + 1)]]
                P.emit(V, lambda e, g=g, stg=stg: e.tensor_copy(
                    stg[:, 1024 * g:1024 * (g + 1)],
                    PS2[0:64, 1024 * g:1024 * (g + 1)]),
                    bws + (stage_war[ssl] if g == 0 else []), [(s_vec, 1)])
                evac_done[(h, g)] = [s_vec.snap()]
                P.emit(V, lambda e, g=g, ssl=ssl, h=h: e.tensor_scalar(
                    dnm(ssl)[:, 1024 * g:1024 * (g + 1)],
                    PS2[64:65, 1024 * g:1024 * (g + 1)],
                    soff64[64:65, c.H + h:c.H + h + 1], None, ALU.add),
                    bws + [eoff_w] + (dnm_war[ssl] if g == 0 else []),
                    [(s_vec, 1)])
                evac_done[(h, g)].append(s_vec.snap())
                dws.append(s_vec.snap())
                bp_war[2 * g] = list(evac_done[(h, g)])
                bp_war[2 * g + 1] = list(evac_done[(h, g)])
            rt = rtile(ssl)
            P.dma(SY, rt, dnm(ssl), s_sydma, waits=dws)
            P.emit(V, lambda e, rt=rt: e.reciprocal(rt, rt),
                   [s_sydma.snap()], [(s_vec, 1)])
            P.dma(SY, rec(ssl), rt, s_sydma,
                  waits=[s_vec.snap()] + dnm_war[ssl])
            recip_done[h] = s_sydma.snap()

        def bc_mult_store(h):
            ssl = h % 2
            stg = a3(STG_COL + ssl * 2048, 2048)[0:64, :]
            mws = []
            for q in range(4):
                si = ps1_slot()
                bca = PS1[0:64, si * 512:si * 512 + 512]
                P.emit(T, lambda e, q=q, ssl=ssl, bca=bca: e.matmul(
                    bca, onesr[0:1, 0:64], rec(ssl, 512 * q, 512),
                    start=True, stop=True, skip_group_check=True),
                    slot_war[si] + ([recip_done[h]] if q == 0 else []),
                    [(s_ten, 1)])
                bcw = s_ten.snap()
                P.emit(V, lambda e, q=q, stg=stg, bca=bca: e.tensor_tensor(
                    stg[:, 512 * q:512 * (q + 1)],
                    stg[:, 512 * q:512 * (q + 1)], bca, ALU.mult),
                    [bcw], [(s_vec, 1)])
                slot_war[si] = [s_vec.snap()]
                mult_done[(h, q)] = s_vec.snap()
                mws.append(mult_done[(h, q)])
            dnm_war[ssl] = [s_ten.snap()]
            k, par = h // 2, h % 2
            dst = AP(attnT_d.tensor,
                     b * c.NT * 128 * c.NK * 128 + 64 * par * 2048 + k * 128,
                     [[2048, 64], [128 * c.NK * 128, c.NT], [1, 128]])
            stw = st_at.load(SC, dst,
                             stg.rearrange("p (t f) -> p t f", f=128),
                             waits=mws)
            stage_war[ssl] = [stw]

        for h in range(c.H):
            if h % 4 == 0:
                vproj(h // 4)
            if h % 2 == 0:
                qkproj(h // 2)
            ebslot = st_eb.k % 2
            eb_col = EB_COL + ebslot * c.C_EB
            ebw = st_eb.emit_load(
                G, lambda e, eb_col=eb_col, h=h: e.indirect_dma_start(
                    a3(eb_col, c.C_EB), None,
                    explb_d.rearrange("h n -> (h n)").unsqueeze(1),
                    IndirectOffsetOnAxis(dgidx[:, 0:1], 0),
                    element_offset=h * (c.PADN + c.S)),
                waits=[explb_w, boot_w] + (ent if h == 0 else []))
            if h == c.H - 1:
                lnfree = s_ten.snap()
                for pc in range(4):
                    P.dma(G, a2(pc * 4 * c.DM, 4 * c.DM).rearrange(
                        "p (k f) -> p k f", f=c.DM),
                        AP(woT_d.tensor, pc * 4 * 128 * c.DM,
                           [[c.DM, 128], [128 * c.DM, 4], [1, c.DM]]),
                        s_woL, waits=[lnfree])
                    woL_ws.append(s_woL.snap())
            first_mm = [True]
            for J in range(c.NK):
                s_strip(h, J, eb_col, ebw, first_mm)
                if J >= 3:
                    pv_strip(h, J - 3)
                if J == 6 and h >= 1:
                    bc_mult_store(h - 1)
            st_eb.set_war([s_vec.snap()])
            for J in range(c.NK - 3, c.NK):
                pv_strip(h, J)
            head_tail(h)
        bc_mult_store(c.H - 1)
        return st_at.total(), woL_ws

    def wo_phase(b, store_ws, piece_ws):
        zst_w = {0: [], 1: []}
        psum_war = {0: [], 1: []}
        wo_entry = [s_vec.snap(), s_scl.snap()]
        for t in range(c.NT):
            sl = t % 2
            att = a1(sl * c.DM, c.DM)
            atw = st_x.load(SY, att, attnT_d[b][t],
                            waits=(list(store_ws) + wo_entry)
                            if t < 2 else [])
            xt = a3f(sl * c.DM, c.DM)
            xw = st_ln.load(SY, xt, x_d[b][t * 128:(t + 1) * 128, :],
                            waits=wo_entry if t < 2 else [])
            yps = (PS1 if sl == 0 else PS2)[:, 0:c.DM]
            for k in range(c.NK):
                for ns in range(4):
                    ws = []
                    if k == 0 and ns == 0:
                        ws = [atw] + psum_war[sl] + \
                            (wo_entry if t < 2 else [])
                    if t == 0 and ns == 0 and k % 4 == 0:
                        ws = ws + [piece_ws[k // 4]]
                    P.emit(T, lambda e, k=k, ns=ns, att=att, yps=yps:
                           e.matmul(
                        yps[:, ns * 512:(ns + 1) * 512],
                        att[:, k * 128:(k + 1) * 128],
                        a2(k * c.DM + ns * 512, 512),
                        start=(k == 0), stop=(k == c.NK - 1),
                        skip_group_check=True),
                        ws, [(s_ten, 1)] if (k == c.NK - 1 and ns == 3)
                        else [])
            z1t = a3f(2 * c.DM + sl * c.DM, c.DM)
            P.emit(V, lambda e, z1t=z1t, xt=xt, yps=yps:
                   e.scalar_tensor_tensor(
                z1t, yps, 1.0, xt, ALU.mult, ALU.add),
                [s_ten.snap(), xw] + zst_w[sl], [(s_vec, 1)])
            psum_war[sl] = [s_vec.snap()]
            st_x.set_war([s_ten.snap()])
            st_ln.set_war([s_vec.snap()])
            zst_w[sl] = [st_z.load(SC, z1_d[b][t * 128:(t + 1) * 128, :],
                                   z1t, waits=[s_vec.snap()])]
        return s_vec.snap()
    def shared_phase(b, ln2_w, entry):
        potst_w = {0: []}
        shT_col = 0
        assert c.NIS * c.S <= AR1W
        POT_B = AR3W - 2 * c.DM
        TMP_COL = POT_B
        assert TMP_COL + c.S <= AR3W and TMP_COL >= 4 * c.ISH
        for isc in range(c.NIS):
            wgb = a3((st_w.k % 2) * c.ISH, c.NK * 128)
            wgw = st_w.load(G, wgb.rearrange("p (k f) -> p k f", f=128),
                            AP(swg_d.tensor, isc * 128,
                               [[c.ISH, 128], [128 * c.ISH, c.NK], [1, 128]]),
                            waits=list(entry) if isc == 0 else [])
            wub = a3(2 * c.ISH + (st_w2.k % 2) * c.ISH, c.NK * 128)
            wuw = st_w2.load(G, wub.rearrange("p (k f) -> p k f", f=128),
                             AP(swu_d.tensor, isc * 128,
                                [[c.ISH, 128], [128 * c.ISH, c.NK],
                                 [1, 128]]),
                             waits=list(entry) if isc == 0 else [])
            gps = PS1[:, 0:min(c.S, 2048)]
            ups = PS2[:, 0:min(c.S, 2048)]
            nstr = cdiv(min(c.S, 2048), 512)
            for k in range(c.NK):
                for ns in range(nstr):
                    n0, n1 = ns * 512, min(c.S, (ns + 1) * 512)
                    P.emit(T, lambda e, k=k, n0=n0, n1=n1, wgb=wgb: e.matmul(
                        gps[:, n0:n1], wgb[:, k * 128:(k + 1) * 128],
                        a2(k * c.S + n0, n1 - n0),
                        start=(k == 0), stop=(k == c.NK - 1)),
                        [wgw, ln2_w, s_scl.snap()] if (k == 0 and ns == 0)
                        else [],
                        [(s_ten, 1)] if (k == c.NK - 1 and ns == nstr - 1)
                        else [])
            g_done = s_ten.snap()
            st_w.set_war(g_done)
            for k in range(c.NK):
                for ns in range(nstr):
                    n0, n1 = ns * 512, min(c.S, (ns + 1) * 512)
                    P.emit(T, lambda e, k=k, n0=n0, n1=n1, wub=wub: e.matmul(
                        ups[:, n0:n1], wub[:, k * 128:(k + 1) * 128],
                        a2(k * c.S + n0, n1 - n0),
                        start=(k == 0), stop=(k == c.NK - 1)),
                        [wuw, s_vec.snap()] if (k == 0 and ns == 0) else [],
                        [(s_ten, 1)] if (k == c.NK - 1 and ns == nstr - 1)
                        else [])
            u_done = s_ten.snap()
            st_w2.set_war(u_done)
            tmp = a3(TMP_COL, c.S)
            P.emit(SC, lambda e, tmp=tmp: e.activation(
                tmp, PS1[:, 0:c.S], AF.Silu),
                [g_done, s_vec.snap()], [(s_scl, 1)])
            P.emit(V, lambda e, tmp=tmp, isc=isc: e.tensor_tensor(
                a1(shT_col + isc * c.S, c.S), tmp, PS2[:, 0:c.S], ALU.mult),
                [u_done, s_scl.snap()] +
                (list(entry) if isc == 0 else []), [(s_vec, 1)])
        sh_done = s_vec.snap()
        n_ar2 = min(c.NIS, AR2W // c.DM)
        assert (c.NIS - n_ar2) * c.DM <= POT_B
        swd_cw = []
        for isc in range(c.NIS):
            dst = (a2(isc * c.DM, c.DM) if isc < n_ar2
                   else a3((isc - n_ar2) * c.DM, c.DM))
            w = st_swd.load(G, dst, AP(swd_d.tensor, isc * 128 * c.DM,
                                       [[c.DM, 128], [1, c.DM]]),
                            waits=[sh_done] if isc < 8 else [])
            st_swd.set_war(w)
            swd_cw.append(w)
        for t in range(c.NT):
            yps = PS1[:, 0:min(c.DM, 2048)]
            war = s_vec.snap()
            nstr = cdiv(min(c.DM, 2048), 512)
            for isc in range(c.NIS):
                rhs = (a2(isc * c.DM, c.DM) if isc < n_ar2
                       else a3((isc - n_ar2) * c.DM, c.DM))
                for ns in range(nstr):
                    n0, n1 = ns * 512, min(c.DM, (ns + 1) * 512)
                    P.emit(T, lambda e, isc=isc, n0=n0, n1=n1, t=t, rhs=rhs:
                           e.matmul(
                        yps[:, n0:n1],
                        a1(shT_col + isc * c.S + t * 128, 128),
                        rhs[:, n0:n1],
                        start=(isc == 0), stop=(isc == c.NIS - 1)),
                        ([sh_done, war] if (isc == 0 and ns == 0) else []) +
                        ([swd_cw[isc]] if (t == 0 and ns == 0) else []),
                        [(s_ten, 1)] if (isc == c.NIS - 1 and
                                         ns == nstr - 1) else [])
            pot = a3f(POT_B // 2, c.DM)
            P.emit(V, lambda e, pot=pot: e.tensor_copy(pot, yps),
                   [s_ten.snap()] + potst_w[0], [(s_vec, 1)])
            ng = b * c.NT + t
            potst_w[0] = [st_po.load(
                SC, preout_d[ng * 128:(ng + 1) * 128, :], pot,
                waits=[s_vec.snap()])]
        return s_vec.snap()

    def routing_phase(logits_w):
        l3 = ltok[:].rearrange("p (f e) -> p f e", e=c.E)
        p3 = probs[:].rearrange("p (f e) -> p f e", e=c.E)
        P.emit(V, lambda e: e.reduce_max(gmax[:].unsqueeze(2), l3,
                                         axis=AX.X),
               [logits_w, s_vec.snap()], [(s_vec, 1)])
        P.emit(V, lambda e: e.tensor_sub(
            p3, l3, gmax[:].unsqueeze(2).to_broadcast([128, c.NF, c.E])),
            [s_vec.snap()], [(s_vec, 1)])
        P.emit(SC, lambda e: e.activation(probs[:], probs[:], AF.Exp),
               [s_vec.snap()], [(s_scl, 1)])
        P.emit(V, lambda e: e.reduce_sum(gsum[:].unsqueeze(2), p3,
                                         axis=AX.X),
               [s_scl.snap()], [(s_vec, 1)])
        P.emit(V, lambda e: e.reciprocal(gsum[:], gsum[:]),
               [s_vec.snap()], [(s_vec, 1)])
        P.emit(V, lambda e: e.tensor_tensor(
            p3, p3, gsum[:].unsqueeze(2).to_broadcast([128, c.NF, c.E]),
            ALU.mult), [s_vec.snap()], [(s_vec, 1)])
        for f in range(c.NF):
            P.emit(V, lambda e, f=f: e.max(
                top8[:], probs[:, f * c.E:(f + 1) * c.E]),
                [s_vec.snap()], [(s_vec, 1)])
            P.emit(V, lambda e, f=f: e.max_index(
                idx8[:], top8[:], probs[:, f * c.E:(f + 1) * c.E]),
                [s_vec.snap()], [(s_vec, 1)])
            P.emit(V, lambda e, f=f: e.tensor_copy(
                w12[:].rearrange("p (c f) -> p c f", f=c.NF)[:, :, f],
                top8[:, 0:2]), [s_vec.snap()], [(s_vec, 1)])
            P.emit(V, lambda e, f=f: e.tensor_copy(
                i12f[:].rearrange("p (c f) -> p c f", f=c.NF)[:, :, f],
                idx8[:, 0:2]), [s_vec.snap()], [(s_vec, 1)])
        P.emit(V, lambda e: e.memset(pos12[:], 0.0), [s_vec.snap()],
               [(s_vec, 1)])
        pps = PS1[:, 0:c.NF]
        cps = PS1[0:1, 512:512 + c.NF]
        ctp = PS1[0:c.NF, 1024:1025]
        crp = PS1[0:1, 1536:1536 + c.NF]
        i1v = i12f[:].rearrange("p (c f) -> p c f", f=c.NF)
        p12v = pos12[:].rearrange("p (c f) -> p c f", f=c.NF)
        for ee in range(c.E):
            P.emit(V, lambda e, ee=ee: e.tensor_single_scalar(
                msk1[:], i1v[:, 0, :], float(ee), ALU.is_equal),
                [s_vec.snap()], [(s_vec, 1)])
            P.emit(V, lambda e, ee=ee: e.tensor_single_scalar(
                mske[:], i1v[:, 1, :], float(ee), ALU.is_equal),
                [s_vec.snap()], [(s_vec, 1)])
            P.emit(V, lambda e: e.tensor_add(mske[:], mske[:], msk1[:]),
                   [s_vec.snap()], [(s_vec, 1)])
            mw = s_vec.snap()
            P.emit(T, lambda e: e.matmul(pps, triu[:], mske[:],
                                         start=True, stop=False,
                                         skip_group_check=True),
                   [mw, boot_w, s_vec.snap()], [(s_ten, 1)])
            P.emit(T, lambda e: e.matmul(cps, onesc[:], mske[:],
                                         start=True, stop=True,
                                         skip_group_check=True),
                   [], [(s_ten, 1)])
            P.emit(V, lambda e: e.tensor_copy(csum[:], cps),
                   [s_ten.snap()], [(s_vec, 1)])
            P.emit(T, lambda e: e.transpose(ctp, csum[:], identf[0:1, 0:1]),
                   [s_vec.snap()], [(s_ten, 1)])
            P.emit(V, lambda e: e.tensor_copy(csumT[0:c.NF, :], ctp),
                   [s_ten.snap()], [(s_vec, 1)])
            P.emit(T, lambda e: e.matmul(crp, csumT[0:c.NF, :],
                                         triu[0:c.NF, 0:c.NF],
                                         start=True, stop=True,
                                         skip_group_check=True),
                   [s_vec.snap()], [(s_ten, 1)])
            P.emit(V, lambda e: e.tensor_copy(carrys[:], crp),
                   [s_ten.snap()], [(s_vec, 1)])
            P.emit(T, lambda e: e.matmul(pps, onesr[:], carrys[:],
                                         start=False, stop=True,
                                         skip_group_check=True),
                   [s_vec.snap()], [(s_ten, 1)])
            P.emit(V, lambda e: e.tensor_copy(pose[:], pps),
                   [s_ten.snap()], [(s_vec, 1)])
            for cc in range(2):
                P.emit(V, lambda e, ee=ee, cc=cc: e.tensor_single_scalar(
                    msk1[:], i1v[:, cc, :], float(ee), ALU.is_equal),
                    [s_vec.snap()], [(s_vec, 1)])
                P.emit(V, lambda e: e.tensor_mul(msk1[:], msk1[:], pose[:]),
                       [s_vec.snap()], [(s_vec, 1)])
                P.emit(V, lambda e, cc=cc: e.tensor_add(
                    p12v[:, cc, :], p12v[:, cc, :], msk1[:]),
                    [s_vec.snap()], [(s_vec, 1)])
        P.emit(V, lambda e: e.tensor_single_scalar(
            pos12[:], pos12[:], float(c.CAP - 1), ALU.min),
            [s_vec.snap()], [(s_vec, 1)])
        P.emit(V, lambda e: e.scalar_tensor_tensor(
            scatf[:], i12f[:], float(c.CAP), pos12[:], ALU.mult, ALU.add),
            [s_vec.snap()], [(s_vec, 1)])
        P.emit(V, lambda e: e.tensor_copy(scat[:], scatf[:]),
               [s_vec.snap()], [(s_vec, 1)])
        return s_vec.snap()

    def scatter_phase(route_w, ln2d_ws, entry):
        scv = scat[:].rearrange("p (c f) -> p c f", f=c.NF)
        for f in range(c.NF):
            lt = a3((f % 2) * c.DM, c.DM)
            lw = st_ln.load(SY, lt, ln2t_d[f * 128:(f + 1) * 128, :],
                            waits=(list(ln2d_ws) + list(entry))
                            if f == 0 else [])
            for cc in range(2):
                scw = st_sc.emit_load(G, lambda e, lt=lt, f=f, cc=cc:
                                      e.indirect_dma_start(
                    expin_d, IndirectOffsetOnAxis(scv[:, cc, f:f + 1], 0),
                    lt, None),
                    waits=[lw, route_w] +
                    ([expin_zero_w] if f == 0 and cc == 0 else []))
                st_sc.set_war(scw)
            st_ln.set_war(st_sc.total())
        return st_sc.total()

    def expert_phase(scat_ws, entry):
        eost_w = {0: [], 1: []}
        XG_COL = c.NI * c.CAP
        EOS_COL = XG_COL + 2 * c.DM
        WD1_COL = EOS_COL + 2 * c.DM
        n_wd1 = min(c.NI, (AR1W - WD1_COL) // c.DM)
        assert 4 * c.DM + c.CAP <= AR3W
        assert c.NK * c.CAP <= AR2W
        assert (c.NI - n_wd1) * c.DM <= AR2W - c.NK * c.CAP

        def wd_ap(isc):
            if isc < n_wd1:
                return a1(WD1_COL + isc * c.DM, c.DM)
            return a2(c.NK * c.CAP + (isc - n_wd1) * c.DM, c.DM)
        prev_down_w = []
        for ee in range(c.E):
            for isc in range(c.NI):
                P.dma(G, wd_ap(isc),
                      AP(wd_d.tensor, ee * c.DI * c.DM + isc * 128 * c.DM,
                         [[c.DM, 128], [1, c.DM]]),
                      s_wdL,
                      waits=(prev_down_w + (list(entry) if ee == 0 else []))
                      if isc == 0 else [])
            wd_w = s_wdL.snap()
            for tt in range(c.NCAP):
                xg = a1(XG_COL + (st_xg.k % 2) * c.DM, c.DM)
                xw = st_xg.load(SY, xg,
                                expin_d[ee * c.CAP + tt * 128:
                                        ee * c.CAP + (tt + 1) * 128, :],
                                waits=(list(scat_ws) + list(entry))
                                if (ee == 0 and tt == 0) else [])
                evw = transpose_tile(
                    lambda k, xg=xg: xg[:, k * 128:(k + 1) * 128],
                    AR2, 0, c.CAP, tt * 128, c.NK, [xw] + (
                        list(entry) if (ee == 0 and tt == 0) else []))
                st_xg.set_war(s_ten.snap())
            xgT_w = s_vec.snap()
            for isc in range(c.NI):
                wgb = a3((st_w.k % 2) * c.DM, c.DM)
                wgw = st_w.load(G, wgb.rearrange("p (k f) -> p k f", f=128),
                                AP(wg_d.tensor, ee * c.DM * c.DI + isc * 128,
                                   [[c.DI, 128], [128 * c.DI, c.NK],
                                    [1, 128]]),
                                waits=list(entry) if (ee == 0 and isc == 0)
                                else [])
                wub = a3(2 * c.DM + (st_w2.k % 2) * c.DM, c.DM)
                wuw = st_w2.load(G, wub.rearrange("p (k f) -> p k f", f=128),
                                 AP(wu_d.tensor, ee * c.DM * c.DI +
                                    isc * 128,
                                    [[c.DI, 128], [128 * c.DI, c.NK],
                                     [1, 128]]),
                                 waits=list(entry) if (ee == 0 and isc == 0)
                                 else [])
                gps = PS1[:, 0:min(c.CAP, 2048)]
                ups = PS2[:, 0:min(c.CAP, 1536)]
                assert c.CAP <= 1536
                nstr = cdiv(c.CAP, 512)
                for k in range(c.NK):
                    for ns in range(nstr):
                        n0, n1 = ns * 512, min(c.CAP, (ns + 1) * 512)
                        P.emit(T, lambda e, k=k, n0=n0, n1=n1, wgb=wgb:
                               e.matmul(
                            gps[:, n0:n1], wgb[:, k * 128:(k + 1) * 128],
                            a2(k * c.CAP + n0, n1 - n0),
                            start=(k == 0), stop=(k == c.NK - 1)),
                            [wgw, xgT_w, s_scl.snap()]
                            if (k == 0 and ns == 0) else [],
                            [(s_ten, 1)] if (k == c.NK - 1 and
                                             ns == nstr - 1) else [])
                g_done = s_ten.snap()
                st_w.set_war(g_done)
                for k in range(c.NK):
                    for ns in range(nstr):
                        n0, n1 = ns * 512, min(c.CAP, (ns + 1) * 512)
                        P.emit(T, lambda e, k=k, n0=n0, n1=n1, wub=wub:
                               e.matmul(
                            ups[:, n0:n1], wub[:, k * 128:(k + 1) * 128],
                            a2(k * c.CAP + n0, n1 - n0),
                            start=(k == 0), stop=(k == c.NK - 1)),
                            [wuw, s_vec.snap()] if (k == 0 and ns == 0)
                            else [],
                            [(s_ten, 1)] if (k == c.NK - 1 and
                                             ns == nstr - 1) else [])
                u_done = s_ten.snap()
                st_w2.set_war(u_done)
                tmp = a3(4 * c.DM, c.CAP)
                P.emit(SC, lambda e, tmp=tmp: e.activation(
                    tmp, PS1[:, 0:c.CAP], AF.Silu),
                    [g_done, s_vec.snap()], [(s_scl, 1)])
                P.emit(V, lambda e, tmp=tmp, isc=isc: e.tensor_tensor(
                    a1(isc * c.CAP, c.CAP), tmp, PS2[:, 0:c.CAP], ALU.mult),
                    [u_done, s_scl.snap()] +
                    (list(entry) if (ee == 0 and isc == 0) else []),
                    [(s_vec, 1)])
            h_done = s_vec.snap()
            for tt in range(c.NCAP):
                yps = PS1[:, 0:min(c.DM, 2048)]
                war = s_vec.snap()
                nstr = cdiv(min(c.DM, 2048), 512)
                for isc in range(c.NI):
                    for ns in range(nstr):
                        n0, n1 = ns * 512, min(c.DM, (ns + 1) * 512)
                        P.emit(T, lambda e, isc=isc, n0=n0, n1=n1, tt=tt:
                               e.matmul(
                            yps[:, n0:n1],
                            a1(isc * c.CAP + tt * 128, 128),
                            wd_ap(isc)[:, n0:n1],
                            start=(isc == 0), stop=(isc == c.NI - 1)),
                            [wd_w, h_done, war] if (isc == 0 and ns == 0)
                            else [],
                            [(s_ten, 1)] if (isc == c.NI - 1 and
                                             ns == nstr - 1) else [])
                sl = st_eo.k % 2
                eos = a1(EOS_COL + sl * c.DM, c.DM)
                P.emit(V, lambda e, eos=eos: e.tensor_copy(eos, yps),
                       [s_ten.snap()] + eost_w[sl], [(s_vec, 1)])
                eost_w[sl] = [st_eo.load(
                    SC, eo_d[ee * c.CAP + tt * 128:
                             ee * c.CAP + (tt + 1) * 128, :], eos,
                    waits=[s_vec.snap()])]
            prev_down_w = [s_ten.snap()]
        return st_eo.total()

    def final_phase(eo_ws, preout_ws, z1_ws, route_w, entry):
        otst_w = {0: [], 1: []}
        scv = scat[:].rearrange("p (c f) -> p c f", f=c.NF)
        for f in range(c.NF):
            b, t = f // c.NT, f % c.NT
            pot = a3f((f % 2) * c.DM, c.DM)
            pw = st_x.load(SY, pot, preout_d[f * 128:(f + 1) * 128, :],
                           waits=(list(preout_ws) + list(entry))
                           if f == 0 else [])
            z1t = a3f(2 * c.DM + (f % 2) * c.DM, c.DM)
            zw = st_ln.load(SY, z1t, z1_d[b][t * 128:(t + 1) * 128, :],
                            waits=list(z1_ws) if f == 0 else [])
            g1 = a1((st_ga.k % 2) * c.DM, c.DM)
            gw1 = st_ga.emit_load(G, lambda e, g1=g1, f=f:
                                  e.indirect_dma_start(
                g1, None, eo_d,
                IndirectOffsetOnAxis(scv[:, 0, f:f + 1], 0)),
                waits=(list(eo_ws) + [route_w] + list(entry))
                if f == 0 else [])
            g2 = a1(2 * c.DM + (st_gb.k % 2) * c.DM, c.DM)
            gw2 = st_gb.emit_load(G, lambda e, g2=g2, f=f:
                                  e.indirect_dma_start(
                g2, None, eo_d,
                IndirectOffsetOnAxis(scv[:, 1, f:f + 1], 0)),
                waits=(list(eo_ws) + list(entry)) if f == 0 else [])
            ot = a2f((f % 2) * c.DM, c.DM)
            P.emit(V, lambda e, g1=g1, pot=pot, ot=ot, f=f:
                   e.scalar_tensor_tensor(
                ot, g1, w12[:, f:f + 1], pot, ALU.mult, ALU.add),
                [gw1, pw, route_w, s_vec.snap()] + otst_w[f % 2] +
                (list(entry) if f < 2 else []),
                [(s_vec, 1)])
            st_x.set_war(s_vec.snap())
            st_ga.set_war(s_vec.snap())
            P.emit(V, lambda e, g2=g2, ot=ot, f=f: e.scalar_tensor_tensor(
                ot, g2, w12[:, c.NF + f:c.NF + f + 1], ot, ALU.mult,
                ALU.add),
                [gw2, s_vec.snap()], [(s_vec, 1)])
            st_gb.set_war(s_vec.snap())
            P.emit(V, lambda e, z1t=z1t, ot=ot: e.tensor_add(ot, ot, z1t),
                   [zw, s_vec.snap()], [(s_vec, 1)])
            st_ln.set_war(s_vec.snap())
            otst_w[f % 2] = [st_out.load(
                SY, out_d[b][t * 128:(t + 1) * 128, :], ot,
                waits=[s_vec.snap()])]
        P.emit(SY, lambda e: None, st_out.total())

    for b in range(c.NB):
        lnw = ln_phase(b, "ln1",
                       [] if b == 0 else P.barrier() + st_z.total())
        store_ws, piece_ws = attn_batch(
            b, lnw, P.barrier() +
            ([explb_w, expin_zero_w] if b == 0 else []))
        wo_phase(b, store_ws, piece_ws)
    z1_ws = st_z.total()
    for b in range(c.NB):
        ln2w = ln_phase(b, "ln2", P.barrier() + st_z.total() +
                        st_po.total())
        shared_phase(b, ln2w, P.barrier() + st_lnst.total())
    preout_ws = st_po.total()
    logits_w = s_vec.snap()
    ln2d_ws = st_lnst.total()
    route_w = routing_phase(logits_w)
    scat_ws = scatter_phase(route_w, ln2d_ws, P.barrier())
    eo_ws = expert_phase(scat_ws, P.barrier() + list(scat_ws))
    final_phase(eo_ws, preout_ws, z1_ws, route_w, P.barrier())

    with nc.Block() as block:
        P.finalize(block)
    return nc


def host_constants(c: Cfg):
    import ml_dtypes
    half = 32
    inv = 1.0 / (10000.0 ** (np.arange(half, dtype=np.float64) / half))
    t = np.arange(c.S, dtype=np.float64)
    ang = np.outer(inv, t)
    cos, sin = np.cos(ang), np.sin(ang)
    C = np.tile(cos, (4, 1)).astype(np.float32)
    Srow = np.concatenate([-sin, sin, -sin, sin], 0).astype(np.float32)
    return {
        "cst": np.full((128, 1), 1e-5, np.float32),
        "identb": np.eye(128, dtype=np.float32).astype(ml_dtypes.bfloat16),
        "identf": np.eye(128, dtype=np.float32),
        "ropeC": np.ascontiguousarray(C),
        "ropeS": np.ascontiguousarray(Srow),
        "triu": np.triu(np.ones((128, 128), np.float32), 1),
        "onesr": np.ones((1, 128), np.float32),
        "onesc": np.ones((128, 1), np.float32),
        "dgidx": (128 - np.arange(128, dtype=np.uint32)).reshape(128, 1),
    }


def host_inputs(c: Cfg, core, inputs):
    cc = np.ascontiguousarray
    x = inputs["L0_attn_hidden_states"]
    m = {
        "x": cc(x[core * c.NB:(core + 1) * c.NB]),
        "wqT": cc(np.asarray(inputs["L0_attn_Wq"]).T),
        "wkT": cc(np.asarray(inputs["L0_attn_Wk"]).T),
        "wvT": cc(np.asarray(inputs["L0_attn_Wv"]).T),
        "woT": cc(np.asarray(inputs["L0_attn_Wo"]).T),
        "lb": cc(np.asarray(inputs["L0_attn_learnable_bias"])),
        "soff": cc(np.asarray(inputs["L0_attn_softmax_offset"])
                   .reshape(1, c.H)),
        "gwT": cc(np.asarray(inputs["L0_ffn_gate_w"]).T),
        "wg": cc(np.asarray(inputs["L0_ffn_w_gate"])),
        "wu": cc(np.asarray(inputs["L0_ffn_w_up"])),
        "wd": cc(np.asarray(inputs["L0_ffn_w_down"])),
        "swg": cc(np.asarray(inputs["L0_ffn_sw_gate"])),
        "swu": cc(np.asarray(inputs["L0_ffn_sw_up"])),
        "swd": cc(np.asarray(inputs["L0_ffn_sw_down"])),
    }
    m.update(host_constants(c))
    return m




_CFG = Cfg()
_N_CORES = 8


def kernel(**inputs):
    import contextlib
    from concourse.bass_utils import run_bass_kernel_spmd
    c = _CFG
    nc = bass.Bass("TRN2", target_bir_lowering=False)
    ctx = contextlib.ExitStack()
    build_program(c, nc, ctx)
    ctx.close()
    nc.finalize()
    in_maps = [host_inputs(c, core, inputs) for core in range(_N_CORES)]
    res = run_bass_kernel_spmd(nc, in_maps, core_ids=list(range(_N_CORES)))
    outs = [np.asarray(r["out"]).reshape(c.NB, c.S, c.DM).astype(np.float32)
            for r in res.results]
    return np.concatenate(outs, axis=0)


# version 12
# speedup vs baseline: 1.0008x; 1.0008x over previous
import numpy as np
import concourse.bass as bass
import concourse.mybir as mybir
from concourse.ap import AP
from concourse.bass import IndirectOffsetOnAxis

F32 = mybir.dt.float32
BF16 = mybir.dt.bfloat16
U32 = mybir.dt.uint32
AF = mybir.ActivationFunctionType
ALU = mybir.AluOpType
AX = mybir.AxisListType

ENGS = ("sync", "tensor", "vector", "scalar", "gpsimd")
SY, T, V, SC, G = ENGS


class Cfg:
    def __init__(self, S=2048, DM=2048, NB=2, DI=1408, ISH=2816, CAP=1280,
                 W=1024, E=8):
        self.S, self.DM, self.NB = S, DM, NB
        self.H = DM // 64
        self.E, self.DI, self.ISH, self.CAP, self.W = E, DI, ISH, CAP, W
        self.NT = S // 128
        self.NK = DM // 128
        self.NTOK = NB * S
        self.NF = self.NTOK // 128
        self.NIG = S // 512
        self.C_EB = 896 + min(W, max(0, S - 512))
        self.PADN = 512
        self.NI = DI // 128
        self.NIS = ISH // 128
        self.NCAP = CAP // 128
        assert S % 512 == 0 and DM % 128 == 0 and DI % 128 == 0
        assert ISH % 128 == 0 and CAP % 128 == 0 and W % 128 == 0
        assert self.NF <= 128 and self.E == 8 and self.NK % 4 == 0


class Sem:
    __slots__ = ("h", "n", "name")

    def __init__(self, h, name):
        self.h, self.n, self.name = h, 0, name

    def snap(self):
        return (self, self.n)


class Prog:
    def __init__(self, nc, ctx):
        self.nc, self.ctx = nc, ctx
        self.stream = {e: [] for e in ENGS}

    def sem(self, name):
        return Sem(self.ctx.enter_context(self.nc.semaphore(name)), name)

    def emit(self, eng, fn, waits=(), incs=()):
        ws = [(s, int(v)) for (s, v) in waits if s is not None]
        ics = []
        for (s, k) in incs:
            s.n += k
            ics.append((s, k))
        self.stream[eng].append((fn, ws, ics))

    def dma(self, q, out, in_, sem, waits=(), **kw):
        self.emit(q, lambda e, o=out, i=in_, kw=kw: e.dma_start(o, i, **kw),
                  waits, [(sem, 16)])

    def barrier(self):
        return [s.snap() for s in (self.s_ten, self.s_vec, self.s_scl,
                                   self.s_gpe)]

    def finalize(self, block):
        for eng in ENGS:
            ops = self.stream[eng]
            if not ops:
                continue

            def body(e, ops=ops):
                for fn, ws, ics in ops:
                    for s, v in ws:
                        e.wait_ge(s.h, v)
                    inst = fn(e)
                    if inst is None:
                        continue
                    for s, k in ics:
                        inst.then_inc(s.h, k)
            getattr(block, eng)(body)


class Stream:

    def __init__(self, P, name, nslots=2):
        self.P, self.nslots = P, nslots
        self.ld = [P.sem(f"{name}_l{i}") for i in range(nslots)]
        self.k = 0
        self.war = [[] for _ in range(nslots)]

    def load(self, q, out, in_, waits=(), **kw):
        i = self.k % self.nslots
        ws = list(waits) + list(self.war[i])
        self.P.dma(q, out, in_, self.ld[i], waits=ws, **kw)
        self.k += 1
        return (self.ld[i], 16 * ((self.k - 1) // self.nslots + 1))

    def emit_load(self, eng, fn, waits=()):
        i = self.k % self.nslots
        ws = list(waits) + list(self.war[i])
        self.P.emit(eng, fn, ws, [(self.ld[i], 16)])
        self.k += 1
        return (self.ld[i], 16 * ((self.k - 1) // self.nslots + 1))

    def set_war(self, snap):
        w = snap if isinstance(snap, list) else [snap]
        self.war[(self.k - 1) % self.nslots] = list(w)

    def total(self):
        return [(self.ld[i], 16 * ((self.k - 1 - i) // self.nslots + 1))
                for i in range(min(self.k, self.nslots))]


def cdiv(a, b):
    return (a + b - 1) // b


def build_program(c: Cfg, nc, ctx):
    P = Prog(nc, ctx)
    dt = nc.dram_tensor
    AR1W, AR2W, AR3W = 45184, c.NK * c.S, 16384

    x_d = dt("x", [c.NB, c.S, c.DM], F32, kind="ExternalInput").ap()
    wqT_d = dt("wqT", [c.DM, c.DM], F32, kind="ExternalInput").ap()
    wkT_d = dt("wkT", [c.DM, c.DM], F32, kind="ExternalInput").ap()
    wvT_d = dt("wvT", [c.DM, c.DM], F32, kind="ExternalInput").ap()
    woT_d = dt("woT", [c.DM, c.DM], F32, kind="ExternalInput").ap()
    lb_d = dt("lb", [c.H, c.W], F32, kind="ExternalInput").ap()
    soff_d = dt("soff", [1, c.H], F32, kind="ExternalInput").ap()
    gwT_d = dt("gwT", [c.DM, c.E], F32, kind="ExternalInput").ap()
    wg_d = dt("wg", [c.E, c.DM, c.DI], F32, kind="ExternalInput").ap()
    wu_d = dt("wu", [c.E, c.DM, c.DI], F32, kind="ExternalInput").ap()
    wd_d = dt("wd", [c.E, c.DI, c.DM], F32, kind="ExternalInput").ap()
    swg_d = dt("swg", [c.DM, c.ISH], F32, kind="ExternalInput").ap()
    swu_d = dt("swu", [c.DM, c.ISH], F32, kind="ExternalInput").ap()
    swd_d = dt("swd", [c.ISH, c.DM], F32, kind="ExternalInput").ap()
    cst_d = dt("cst", [128, 1], F32, kind="ExternalInput").ap()
    identb_d = dt("identb", [128, 128], BF16, kind="ExternalInput").ap()
    identf_d = dt("identf", [128, 128], F32, kind="ExternalInput").ap()
    ropeC_d = dt("ropeC", [128, c.S], F32, kind="ExternalInput").ap()
    ropeS_d = dt("ropeS", [128, c.S], F32, kind="ExternalInput").ap()
    triu_d = dt("triu", [128, 128], F32, kind="ExternalInput").ap()
    onesr_d = dt("onesr", [1, 128], F32, kind="ExternalInput").ap()
    onesc_d = dt("onesc", [128, 1], F32, kind="ExternalInput").ap()
    dgidx_d = dt("dgidx", [128, 1], U32, kind="ExternalInput").ap()
    iota_d = dt("iota", [128, 1], F32, kind="ExternalInput").ap()
    out_d = dt("out", [c.NB, c.S, c.DM], F32, kind="ExternalOutput").ap()

    z1_d = dt("z1s", [c.NB, c.S, c.DM], F32, kind="Internal").ap()
    attnT_d = dt("attnTs", [c.NB, c.NT, 128, c.NK * 128], BF16,
                 kind="Internal").ap()
    explb_d = dt("explbs", [c.H, c.PADN + c.S], F32, kind="Internal").ap()
    ln2t_d = dt("ln2ts", [c.NTOK, c.DM], BF16, kind="Internal").ap()
    preout_d = dt("preouts", [c.NTOK, c.DM], BF16, kind="Internal").ap()
    ginv_d = dt("ginvs", [c.E * c.CAP, 1], U32, kind="Internal").ap()
    sinv_d = dt("sinvs", [c.E * c.CAP, 1], U32, kind="Internal").ap()
    wslot_d = dt("wslots", [c.E * c.CAP, 1], F32, kind="Internal").ap()
    contrib_d = dt("contribs", [2 * c.NTOK + 128, c.DM], BF16,
                   kind="Internal").ap()

    sbt = lambda n, sh, d: ctx.enter_context(nc.sbuf_tensor(n, sh, d))
    AR1 = sbt("AR1", [128, AR1W], BF16)
    AR2 = sbt("AR2", [128, AR2W], BF16)
    AR3 = sbt("AR3", [128, AR3W], BF16)

    def a1(col, n):
        return AR1[:, col:col + n]

    def a2(col, n):
        return AR2[:, col:col + n]

    def a3(col, n):
        return AR3[:, col:col + n]

    def a1f(col, n):
        return AR1[:, 2 * col:2 * (col + n)].bitcast(F32)

    def a2f(col, n):
        return AR2[:, 2 * col:2 * (col + n)].bitcast(F32)

    def a3f(col, n):
        return AR3[:, 2 * col:2 * (col + n)].bitcast(F32)

    cstv = sbt("cstvs", [128, 1], F32)
    identb = sbt("identbs", [128, 128], BF16)
    identf = sbt("identfs", [128, 128], F32)
    triu = sbt("trius", [128, 128], F32)
    onesr = sbt("onesrs", [1, 128], F32)
    onesc = sbt("onescs", [128, 1], F32)
    dgidx = sbt("dgidxs", [128, 1], U32)
    iotas = sbt("iotass", [128, 1], F32)
    invi = sbt("invis", [128, 80], F32)
    inviu = sbt("invius", [128, 80], U32)
    gv = sbt("gvs", [128, 4], U32)
    wsl = sbt("wsls", [128, 2], F32)
    gwT = sbt("gwTs", [128, c.NK * c.E], BF16)
    soff64 = sbt("soff64s", [128, 2 * c.H], F32)
    ropeC = sbt("ropeCs", [128, c.S], BF16)
    ropeS = sbt("ropeSs", [128, c.S], BF16)
    ltok = sbt("ltoks", [128, c.NF * c.E], F32)
    gmax = sbt("gmaxs", [128, c.NF], F32)
    gsum = sbt("gsums", [128, c.NF], F32)
    probs = sbt("probss", [128, c.NF * c.E], F32)
    top8 = sbt("top8s", [128, 8], F32)
    idx8 = sbt("idx8s", [128, 8], U32)
    i12f = sbt("i12fs", [128, 2 * c.NF], F32)
    w12 = sbt("w12s", [128, 2 * c.NF], F32)
    pos12 = sbt("pos12s", [128, 2 * c.NF], F32)
    scatf = sbt("scatfs", [128, 2 * c.NF], F32)
    scat = sbt("scats", [128, 2 * c.NF], U32)
    mske = sbt("mskes", [128, c.NF], F32)
    msk1 = sbt("msk1s", [128, c.NF], F32)
    pose = sbt("poses", [128, c.NF], F32)
    csum = sbt("csums", [1, c.NF], F32)
    csumT = sbt("csumTs", [128, 1], F32)
    carrys = sbt("carryss", [1, c.NF], F32)
    stats = sbt("statss", [128, 8], F32)
    statb = sbt("statbs", [128, 8], F32)

    PS1 = ctx.enter_context(nc.psum_tensor("PS1", [128, 2048], F32))
    PS2 = ctx.enter_context(nc.psum_tensor("PS2", [128, 2048], F32))
    PS2W = 2048
    trps = PS2[:, 1536:2048].bitcast(BF16)

    s_vec = P.sem("vec")
    s_scl = P.sem("scl")
    s_ten = P.sem("ten")
    s_gpe = P.sem("gpe")
    s_sydma = P.sem("sydma")
    s_gpdma = P.sem("gpdma")
    s_exl = P.sem("exl")
    s_zr = P.sem("zr")
    s_woL = P.sem("woL")
    s_swdL = P.sem("swdL")
    s_wdL = P.sem("wdL")
    s_boot = P.sem("boot")
    P.s_vec, P.s_scl, P.s_ten, P.s_gpe = s_vec, s_scl, s_ten, s_gpe

    st_x = Stream(P, "x")
    st_w = Stream(P, "w")
    st_w2 = Stream(P, "w2")
    st_wv = Stream(P, "wv", 1)
    st_at = Stream(P, "at", 2)
    st_swd = Stream(P, "swd", 8)
    st_eb = Stream(P, "eb")
    st_ln = Stream(P, "ln")
    st_ga = Stream(P, "ga")
    st_gb = Stream(P, "gb")
    st_xg = Stream(P, "xg")
    st_z = Stream(P, "z")
    st_lnst = Stream(P, "lnst")
    st_po = Stream(P, "po", 1)
    st_eo = Stream(P, "eo")
    st_sc = Stream(P, "sc")
    st_out = Stream(P, "outs")

    nc.const_aps.register(F32, 1e-5, cstv[:, 0:1])
    for dst, src in ((cstv[:], cst_d), (identb[:], identb_d),
                     (identf[:], identf_d),
                     (triu[:], triu_d), (onesr[:], onesr_d),
                     (onesc[:], onesc_d), (dgidx[:], dgidx_d),
                     (iotas[:], iota_d), (ropeC[:], ropeC_d),
                     (ropeS[:], ropeS_d), (soff64[64:65, 0:c.H], soff_d)):
        P.dma(G, dst, src, s_boot)
    P.dma(G, gwT[:].rearrange("p (k e) -> p k e", e=c.E),
          AP(gwT_d.tensor, 0, [[c.E, 128], [128 * c.E, c.NK], [1, c.E]]),
          s_boot)
    boot_w = s_boot.snap()

    LEX = c.PADN + c.S
    exl = a1f(2048, LEX)[0:c.H, :]
    exl2 = a1f(2048 + LEX, LEX)[0:c.H, :]
    P.emit(G, lambda e: e.memset(exl[:, 0:c.PADN], -1e30), [], [(s_gpe, 1)])
    P.emit(G, lambda e: e.memset(exl[:, c.PADN + c.W:LEX], 0.0), [],
           [(s_gpe, 1)])
    P.dma(G, exl[:, c.PADN:c.PADN + c.W], lb_d, s_exl, waits=[s_gpe.snap()])
    P.emit(SC, lambda e: e.activation(exl2, exl, AF.Exp),
           [s_exl.snap(), s_gpe.snap()], [(s_scl, 1)])
    P.dma(G, explb_d, exl2, s_exl, waits=[s_scl.snap()])
    explb_w = s_exl.snap()

    P.emit(SC, lambda e: e.activation(soff64[64:65, c.H:2 * c.H],
                                      soff64[64:65, 0:c.H], AF.Exp),
           [boot_w], [(s_scl, 1)])
    eoff_w = s_scl.snap()

    P.emit(G, lambda e: e.memset(invi[:], 0.0), [], [(s_gpe, 1)])
    P.emit(V, lambda e: e.tensor_copy(inviu[:], invi[:]),
           [s_gpe.snap()], [(s_vec, 1)])
    P.dma(G, AP(ginv_d.tensor, 0, [[80, 128], [1, 80]]), inviu[:],
          s_zr, waits=[s_vec.snap()])
    P.emit(V, lambda e: e.tensor_scalar(
        invi[:], iotas[:, 0:1].to_broadcast([128, 80]),
        float(2 * c.NTOK), None, ALU.add), [boot_w], [(s_vec, 1)])
    P.emit(V, lambda e: e.tensor_copy(inviu[:], invi[:]),
           [s_vec.snap(), s_zr.snap()], [(s_vec, 1)])
    P.dma(G, AP(sinv_d.tensor, 0, [[80, 128], [1, 80]]), inviu[:],
          s_zr, waits=[s_vec.snap()])
    ZW = 4096
    zt = a1(0, ZW)
    P.emit(G, lambda e: e.memset(zt, 0.0), [], [(s_gpe, 1)])
    ztot = (2 * c.NTOK + 128) * c.DM
    assert ztot % ZW == 0
    zoff = 0
    while zoff < ztot:
        n = min(128 * ZW, ztot - zoff)
        npart = n // ZW
        P.dma(G, AP(contrib_d.tensor, zoff, [[ZW, npart], [1, ZW]]),
              zt[0:npart, :], s_zr, waits=[s_gpe.snap()])
        zoff += n
    expin_zero_w = s_zr.snap()

    def transpose_tile(src_sb_fn, dst_t, dst_col0, dst_stride, t0, nk,
                       first_waits):
        for g0 in range(0, nk, 8):
            gn = min(8, nk - g0)
            for j in range(gn):
                k = g0 + j
                ws = [s_vec.snap()] if j == 0 else []
                if g0 == 0 and j == 0:
                    ws += list(first_waits)
                P.emit(T, lambda e, k=k, j=j: e.transpose(
                    trps[:, j * 128:(j + 1) * 128], src_sb_fn(k), identb[:]),
                    ws, [(s_ten, 1)] if j == gn - 1 else [])

            def ev(e, g0=g0, gn=gn):
                dst = AP(dst_t, dst_col0 + g0 * dst_stride + t0,
                         [[int(dst_t.shape[1]), 128],
                          [dst_stride, gn], [1, 128]])
                src = trps[:, 0:gn * 128].rearrange("p (g f) -> p g f", g=gn)
                return e.tensor_copy(dst, src)
            P.emit(V, ev, [s_ten.snap()], [(s_vec, 1)])
        return s_vec.snap()

    def ln_phase(b, kind, entry):
        evw = None
        lnst_w = {0: [], 1: []}
        src = x_d[b] if kind == "ln1" else z1_d[b]
        lane_w = {0: [], 1: []}
        sq_w = {0: [], 1: []}
        for t in range(c.NT):
            sl = t % 2
            S0, S1, S2, S3 = 4 * sl, 4 * sl + 1, 4 * sl + 2, 4 * sl + 3
            B0, B1, B2 = 2 * sl, 2 * sl + 1, 4 + sl
            xt = a3f(3 * c.DM // 2 + sl * c.DM, c.DM)
            lnt = a3(sl * c.DM, c.DM)
            xw = st_x.load(SY, xt, src[t * 128:(t + 1) * 128, :],
                           waits=list(entry) if t < 2 else [])
            P.emit(V, lambda e, xt=xt, S0=S0: e.reduce_sum(
                stats[:, S0:S0 + 1], xt, axis=AX.X),
                [xw] + lane_w[sl] + sq_w[sl], [(s_vec, 1)])
            sum_w = s_vec.snap()
            P.emit(SC, lambda e, xt=xt, S1=S1: e.activation(
                a3(2 * c.DM, c.DM), xt,
                AF.Square, accum_out=stats[:, S1:S1 + 1]),
                [xw] + lane_w[sl] + sq_w[0] + sq_w[1], [(s_scl, 1)])
            sq_done = s_scl.snap()
            P.emit(V, lambda e, S0=S0, S2=S2: e.tensor_mul(
                stats[:, S2:S2 + 1], stats[:, S0:S0 + 1],
                stats[:, S0:S0 + 1]), [sum_w, s_vec.snap()], [(s_vec, 1)])
            P.emit(V, lambda e, S1=S1, S2=S2, S3=S3: e.scalar_tensor_tensor(
                stats[:, S3:S3 + 1], stats[:, S2:S2 + 1], -1.0 / c.DM,
                stats[:, S1:S1 + 1], ALU.mult, ALU.add),
                [s_vec.snap(), sq_done], [(s_vec, 1)])
            P.emit(SC, lambda e, S3=S3, B0=B0: e.activation(
                statb[:, B0:B0 + 1], stats[:, S3:S3 + 1], AF.Sqrt, bias=1e-5,
                scale=1.0 / c.DM), [s_vec.snap(), boot_w], [(s_scl, 1)])
            P.emit(V, lambda e, B0=B0, B1=B1: e.reciprocal(
                statb[:, B1:B1 + 1], statb[:, B0:B0 + 1]),
                [s_scl.snap()], [(s_vec, 1)])
            P.emit(V, lambda e, S0=S0, B2=B2: e.tensor_scalar(
                statb[:, B2:B2 + 1], stats[:, S0:S0 + 1], -1.0 / c.DM, None,
                ALU.mult), [sum_w, s_vec.snap()], [(s_vec, 1)])
            P.emit(V, lambda e, xt=xt, lnt=lnt, B1=B1, B2=B2: e.tensor_scalar(
                lnt, xt, statb[:, B2:B2 + 1], statb[:, B1:B1 + 1],
                ALU.add, ALU.mult),
                [s_vec.snap()] + lnst_w[sl] + (list(entry) if t < 2 else []),
                [(s_vec, 1)])
            lane_w[sl] = [s_vec.snap()]
            sq_w[sl] = [s_scl.snap()]
            st_x.set_war(s_vec.snap())
            ln_w = s_vec.snap()
            if kind == "ln2":
                ng = b * c.NT + t
                lnst_w[sl] = [st_lnst.load(
                    SC, ln2t_d[ng * 128:(ng + 1) * 128, :], lnt,
                    waits=[ln_w])]
            evw = transpose_tile(
                lambda k, lnt=lnt: lnt[:, k * 128:(k + 1) * 128],
                AR2, 0, c.S, t * 128, c.NK,
                [ln_w] + (list(entry) if t == 0 else []))
            if kind == "ln2":
                pass
            if kind == "ln2":
                ng = b * c.NT + t
                lps = PS1[0:c.E, 256:384]
                for k in range(c.NK):
                    P.emit(T, lambda e, k=k, t=t: e.matmul(
                        lps[:, 0:128], gwT[:, k * c.E:(k + 1) * c.E],
                        a2(k * c.S + t * 128, 128),
                        start=(k == 0), stop=(k == c.NK - 1),
                        skip_group_check=True),
                        [evw, boot_w, s_vec.snap()] if k == 0 else [],
                        [(s_ten, 1)] if k == c.NK - 1 else [])
                lsb = a3f(7 * c.DM // 2, 128)[0:c.E, :]
                P.emit(V, lambda e, lsb=lsb: e.tensor_copy(lsb, lps[:, 0:128]),
                       [s_ten.snap()], [(s_vec, 1)])
                tps = PS2[:, 1280:1280 + c.E]
                P.emit(T, lambda e, lsb=lsb, tps=tps: e.transpose(
                    tps, lsb, identf[0:c.E, 0:c.E]),
                    [s_vec.snap()], [(s_ten, 1)])
                P.emit(V, lambda e, ng=ng, tps=tps: e.tensor_copy(
                    ltok[:, ng * c.E:(ng + 1) * c.E], tps),
                    [s_ten.snap()], [(s_vec, 1)])
        return s_vec.snap()

    PB_COL = 0
    NPSL = 6
    VB_COL = NPSL * 2048
    KB_COL = VB_COL + c.NT * 260
    QB_COL = KB_COL + c.S
    QSB_COL = QB_COL + c.S
    QSW_COL = QSB_COL + c.S
    DNM_F = (QSW_COL + c.S) // 2
    WV_COL = QSW_COL + c.S + 8192
    assert WV_COL + c.NK * 256 <= AR1W
    EB_COL = 4096
    EXPS_COL = EB_COL + 2 * c.C_EB
    STG_COL = EXPS_COL + 4 * 512
    RSCR_F = (STG_COL + 2 * c.DM) // 2
    assert 2 * RSCR_F + 2048 <= AR3W
    BAND = c.W + 128
    WQK_COL = 0

    def attn_batch(b, lnxT_w, entry):
        dr = a1f(DNM_F, 4096)

        def dnm(s):
            return dr[64:65, s * 2048:(s + 1) * 2048]

        def rec(s, col0=0, n=2048):
            return dr[0:1, s * 2048 + col0:s * 2048 + col0 + n]

        def rtile(s):
            return dr[64:128, s * 2048:s * 2048 + 32]

        ent = list(entry)
        slot_war = [ent + [s_vec.snap()] for _ in range(4)]
        exps_war = [ent + [s_vec.snap()] for _ in range(4)]
        pslot_war = [ent + [s_ten.snap()] for _ in range(NPSL)]
        bp_war = {q: ent + [s_vec.snap()] for q in range(4)}
        stage_war = {0: list(ent), 1: list(ent)}
        dnm_war = {0: [], 1: []}
        qsb_war = [[s_vec.snap()]]
        qsw_war = [[s_vec.snap()]]
        piece_ready = {}
        bank_done = {}
        evac_done = {}
        mult_done = {}
        recip_done = {}
        vready = [None]
        qk_w = [None]
        ps1_i = [0]
        exps_i = [0]
        woL_ws = []

        def ps1_slot():
            i = ps1_i[0] % 4
            ps1_i[0] += 1
            return i

        def vproj(sb):
            wv = a1(WV_COL, c.NK * 256)
            wvw = st_wv.load(G, wv.rearrange("p (k f) -> p k f", f=256),
                             AP(wvT_d.tensor, sb * 256,
                                [[c.DM, 128], [128 * c.DM, c.NK], [1, 256]]),
                             waits=(ent + [lnxT_w]) if sb == 0 else [])
            for t in range(c.NT):
                si = ps1_slot()
                sps = PS1[:, si * 512:si * 512 + 256]
                for k in range(c.NK):
                    P.emit(T, lambda e, k=k, t=t, sps=sps, wv=wv: e.matmul(
                        sps, a2(k * c.S + t * 128, 128),
                        wv[:, k * 256:(k + 1) * 256],
                        start=(k == 0), stop=(k == c.NK - 1),
                        skip_group_check=True),
                        ([wvw] + slot_war[si] +
                         ([lnxT_w] if t == 0 else [])) if k == 0 else [],
                        [(s_ten, 1)] if k == c.NK - 1 else [])
                vdone = s_ten.snap()

                def vev(e, t=t, sps=sps):
                    dst = AP(AR1, VB_COL + t * 260,
                             [[AR1W, 128], [65, 4], [1, 64]])
                    return e.tensor_copy(dst, sps.rearrange(
                        "p (h f) -> p h f", h=4))
                P.emit(V, vev, [vdone] + (ent if (sb == 0 and t == 0)
                                          else []), [(s_vec, 1)])
                slot_war[si] = [s_vec.snap()]
            st_wv.set_war([s_ten.snap()])

            def vones(e):
                dst = AP(AR1, VB_COL + 64,
                         [[AR1W, 128], [260, c.NT], [65, 4], [1, 1]])
                return e.memset(dst, 1.0)
            P.emit(V, vones, [s_vec.snap()], [(s_vec, 1)])
            vready[0] = s_vec.snap()

        def qkproj(p):
            for which, wT_d, dcol, kscale in (
                    ("q", wqT_d, QB_COL, 1.0), ("k", wkT_d, KB_COL, 0.125)):
                wslot = st_w2.k % 2
                wqk = a3(WQK_COL + wslot * 2048, 2048)
                ww = st_w2.load(G, wqk.rearrange("p (k f) -> p k f", f=128),
                                AP(wT_d.tensor, p * 128,
                                   [[c.DM, 128], [128 * c.DM, c.NK],
                                    [1, 128]]),
                                waits=(ent + [lnxT_w])
                                if (p == 0 and which == "q") else [])
                qsb = a1(QSB_COL, c.S)
                qsw = a1(QSW_COL, c.S)
                for ns in range(4):
                    si = ps1_slot()
                    sps = PS1[:, si * 512:(si + 1) * 512]
                    for k in range(c.NK):
                        P.emit(T, lambda e, k=k, ns=ns, sps=sps, wqk=wqk:
                               e.matmul(
                            sps, wqk[:, k * 128:(k + 1) * 128],
                            a2(k * c.S + ns * 512, 512),
                            start=(k == 0), stop=(k == c.NK - 1),
                            skip_group_check=True),
                            ([ww] + slot_war[si] +
                             ([lnxT_w] if (p == 0 and ns == 0) else []))
                            if k == 0 else [],
                            [(s_ten, 1)] if k == c.NK - 1 else [])
                    pdone = s_ten.snap()
                    P.emit(V, lambda e, ns=ns, sps=sps, qsb=qsb, ks=kscale:
                           e.tensor_scalar(
                        qsb[:, ns * 512:(ns + 1) * 512], sps, ks, None,
                        ALU.mult),
                        [pdone] + (qsb_war[0] if ns == 0 else []),
                        [(s_vec, 1)])
                    slot_war[si] = [s_vec.snap()]
                st_w2.set_war([s_ten.snap()])
                qsb_w = s_vec.snap()
                for (d0, sc0) in ((0, 32), (32, 0), (64, 96), (96, 64)):
                    P.dma(SY, qsw[d0:d0 + 32, :], qsb[sc0:sc0 + 32, :],
                          s_sydma, waits=[qsb_w] + qsw_war[0])
                dstb = a1(dcol, c.S)
                P.emit(V, lambda e, dstb=dstb, qsb=qsb: e.tensor_mul(
                    dstb, qsb, ropeC[:, 0:c.S]),
                    [boot_w, s_ten.snap(), s_vec.snap()], [(s_vec, 1)])
                P.emit(V, lambda e, qsb=qsb, qsw=qsw: e.tensor_mul(
                    qsb, qsw, ropeS[:, 0:c.S]),
                    [s_sydma.snap()], [(s_vec, 1)])
                qsw_war[0] = [s_vec.snap()]
                P.emit(V, lambda e, dstb=dstb, qsb=qsb: e.tensor_add(
                    dstb, dstb, qsb), [], [(s_vec, 1)])
                qsb_war[0] = [s_vec.snap()]
            qk_w[0] = s_vec.snap()

        def s_strip(h, J, eb_col, ebw, first_mm):
            hh2 = h % 2
            sl = (h * 16 + J) % NPSL
            pw = pslot_war[sl]
            fscl, fvec = [True], [True]
            for q in range(J // 4, 4):
                c0 = max(0, 512 * q - 128 * J)
                c1 = 512 * (q + 1) - 128 * J
                w = c1 - c0
                i0 = 128 * J + c0
                si = ps1_slot()
                sps = PS1[:, si * 512:si * 512 + w]
                P.emit(T, lambda e, J=J, i0=i0, w=w, sps=sps, hh2=hh2:
                       e.matmul(
                    sps, a1(KB_COL + J * 128, 128)[64 * hh2:64 * hh2 + 64, :],
                    a1(QB_COL + i0, w)[64 * hh2:64 * hh2 + 64, :],
                    start=True, stop=True, skip_group_check=True),
                    slot_war[si] + ([qk_w[0]] if first_mm[0] else []),
                    [(s_ten, 1)])
                first_mm[0] = False
                smm = s_ten.snap()
                rdy = []
                ib1 = min(c1, BAND)
                if ib1 > c0:
                    ei = exps_i[0] % 4
                    exps_i[0] += 1
                    es = a3(EXPS_COL + ei * 512, 512)
                    P.emit(SC, lambda e, es=es, sps=sps, n=ib1 - c0:
                           e.activation(es[:, 0:n], sps[:, 0:n], AF.Exp),
                           [smm] + exps_war[ei], [(s_scl, 1)])
                    sw = s_scl.snap()
                    P.emit(V, lambda e, sl=sl, c0=c0, n=ib1 - c0, es=es,
                           eb_col=eb_col: e.tensor_mul(
                        a1(PB_COL + sl * 2048 + c0, n), es[:, 0:n],
                        a3(eb_col + 384 + c0, n)),
                        [sw, ebw] + (pw if fvec[0] else []),
                        [(s_vec, 1)])
                    fvec[0] = False
                    exps_war[ei] = [s_vec.snap()]
                    rdy.append(s_vec.snap())
                if c1 > ib1:
                    o0 = max(c0, ib1)
                    P.emit(SC, lambda e, sl=sl, o0=o0, c0=c0, c1=c1, sps=sps:
                           e.activation(
                        a1(PB_COL + sl * 2048 + o0, c1 - o0),
                        sps[:, o0 - c0:c1 - c0], AF.Exp),
                        [smm] + (pw if fscl[0] else []),
                        [(s_scl, 1)])
                    fscl[0] = False
                    rdy.append(s_scl.snap())
                slot_war[si] = [s_scl.snap()]
                piece_ready[(h, J, q)] = rdy

        def pv_strip(h, J):
            hh4 = h % 4
            sl = (h * 16 + J) % NPSL
            for q in range(J // 4, 4):
                c0 = max(0, 512 * q - 128 * J)
                c1 = 512 * (q + 1) - 128 * J
                o0 = 128 * J + c0
                w = c1 - c0
                start = (J == 0)
                stop = (J == min(4 * q + 3, 15))
                ws = list(piece_ready.pop((h, J, q), []))
                if start:
                    ws += bp_war[q] + [vready[0]]
                P.emit(T, lambda e, hh4=hh4, sl=sl, c0=c0, o0=o0, w=w, J=J,
                       start=start, stop=stop: e.matmul(
                    PS2[0:65, o0:o0 + w],
                    AP(AR1, VB_COL + J * 260 + hh4 * 65,
                       [[AR1W, 128], [1, 65]]),
                    a1(PB_COL + sl * 2048 + c0, w),
                    start=start, stop=stop, skip_group_check=True),
                    ws, [(s_ten, 1)] if (stop or q == 3) else [])
                if stop:
                    bank_done[(h, q)] = s_ten.snap()
            pslot_war[sl] = [s_ten.snap()]

        def head_tail(h):
            ssl = h % 2
            stg = a3(STG_COL + ssl * 2048, 2048)[0:64, :]
            dws = []
            for g in (0, 1):
                bws = [bank_done[(h, 2 * g)], bank_done[(h, 2 * g + 1)]]
                P.emit(V, lambda e, g=g, stg=stg: e.tensor_copy(
                    stg[:, 1024 * g:1024 * (g + 1)],
                    PS2[0:64, 1024 * g:1024 * (g + 1)]),
                    bws + (stage_war[ssl] if g == 0 else []), [(s_vec, 1)])
                evac_done[(h, g)] = [s_vec.snap()]
                P.emit(V, lambda e, g=g, ssl=ssl, h=h: e.tensor_scalar(
                    dnm(ssl)[:, 1024 * g:1024 * (g + 1)],
                    PS2[64:65, 1024 * g:1024 * (g + 1)],
                    soff64[64:65, c.H + h:c.H + h + 1], None, ALU.add),
                    bws + [eoff_w] + (dnm_war[ssl] if g == 0 else []),
                    [(s_vec, 1)])
                evac_done[(h, g)].append(s_vec.snap())
                dws.append(s_vec.snap())
                bp_war[2 * g] = list(evac_done[(h, g)])
                bp_war[2 * g + 1] = list(evac_done[(h, g)])
            rt = rtile(ssl)
            P.dma(SY, rt, dnm(ssl), s_sydma, waits=dws)
            P.emit(V, lambda e, rt=rt: e.reciprocal(rt, rt),
                   [s_sydma.snap()], [(s_vec, 1)])
            P.dma(SY, rec(ssl), rt, s_sydma,
                  waits=[s_vec.snap()] + dnm_war[ssl])
            recip_done[h] = s_sydma.snap()

        def bc_mult_store(h):
            ssl = h % 2
            stg = a3(STG_COL + ssl * 2048, 2048)[0:64, :]
            mws = []
            for q in range(4):
                si = ps1_slot()
                bca = PS1[0:64, si * 512:si * 512 + 512]
                P.emit(T, lambda e, q=q, ssl=ssl, bca=bca: e.matmul(
                    bca, onesr[0:1, 0:64], rec(ssl, 512 * q, 512),
                    start=True, stop=True, skip_group_check=True),
                    slot_war[si] + ([recip_done[h]] if q == 0 else []),
                    [(s_ten, 1)])
                bcw = s_ten.snap()
                P.emit(V, lambda e, q=q, stg=stg, bca=bca: e.tensor_tensor(
                    stg[:, 512 * q:512 * (q + 1)],
                    stg[:, 512 * q:512 * (q + 1)], bca, ALU.mult),
                    [bcw], [(s_vec, 1)])
                slot_war[si] = [s_vec.snap()]
                mult_done[(h, q)] = s_vec.snap()
                mws.append(mult_done[(h, q)])
            dnm_war[ssl] = [s_ten.snap()]
            k, par = h // 2, h % 2
            dst = AP(attnT_d.tensor,
                     b * c.NT * 128 * c.NK * 128 + 64 * par * 2048 + k * 128,
                     [[2048, 64], [128 * c.NK * 128, c.NT], [1, 128]])
            stw = st_at.load(SC, dst,
                             stg.rearrange("p (t f) -> p t f", f=128),
                             waits=mws)
            stage_war[ssl] = [stw]

        for h in range(c.H):
            if h % 4 == 0:
                vproj(h // 4)
            if h % 2 == 0:
                qkproj(h // 2)
            ebslot = st_eb.k % 2
            eb_col = EB_COL + ebslot * c.C_EB
            ebw = st_eb.emit_load(
                G, lambda e, eb_col=eb_col, h=h: e.indirect_dma_start(
                    a3(eb_col, c.C_EB), None,
                    explb_d.rearrange("h n -> (h n)").unsqueeze(1),
                    IndirectOffsetOnAxis(dgidx[:, 0:1], 0),
                    element_offset=h * (c.PADN + c.S)),
                waits=[explb_w, boot_w] + (ent if h == 0 else []))
            if h == c.H - 1:
                lnfree = s_ten.snap()
                for pc in range(4):
                    P.dma(G, a2(pc * 4 * c.DM, 4 * c.DM).rearrange(
                        "p (k f) -> p k f", f=c.DM),
                        AP(woT_d.tensor, pc * 4 * 128 * c.DM,
                           [[c.DM, 128], [128 * c.DM, 4], [1, c.DM]]),
                        s_woL, waits=[lnfree])
                    woL_ws.append(s_woL.snap())
            first_mm = [True]
            for J in range(c.NK):
                s_strip(h, J, eb_col, ebw, first_mm)
                if J >= 3:
                    pv_strip(h, J - 3)
                if J == 6 and h >= 1:
                    bc_mult_store(h - 1)
            st_eb.set_war([s_vec.snap()])
            for J in range(c.NK - 3, c.NK):
                pv_strip(h, J)
            head_tail(h)
        bc_mult_store(c.H - 1)
        return st_at.total(), woL_ws

    def wo_phase(b, store_ws, piece_ws):
        zst_w = {0: [], 1: []}
        psum_war = {0: [], 1: []}
        wo_entry = [s_vec.snap(), s_scl.snap()]
        for t in range(c.NT):
            sl = t % 2
            att = a1(sl * c.DM, c.DM)
            atw = st_x.load(SY, att, attnT_d[b][t],
                            waits=(list(store_ws) + wo_entry)
                            if t < 2 else [])
            xt = a3f(sl * c.DM, c.DM)
            xw = st_ln.load(SY, xt, x_d[b][t * 128:(t + 1) * 128, :],
                            waits=wo_entry if t < 2 else [])
            yps = (PS1 if sl == 0 else PS2)[:, 0:c.DM]
            for k in range(c.NK):
                for ns in range(4):
                    ws = []
                    if k == 0 and ns == 0:
                        ws = [atw] + psum_war[sl] + \
                            (wo_entry if t < 2 else [])
                    if t == 0 and ns == 0 and k % 4 == 0:
                        ws = ws + [piece_ws[k // 4]]
                    P.emit(T, lambda e, k=k, ns=ns, att=att, yps=yps:
                           e.matmul(
                        yps[:, ns * 512:(ns + 1) * 512],
                        att[:, k * 128:(k + 1) * 128],
                        a2(k * c.DM + ns * 512, 512),
                        start=(k == 0), stop=(k == c.NK - 1),
                        skip_group_check=True),
                        ws, [(s_ten, 1)] if (k == c.NK - 1 and ns == 3)
                        else [])
            z1t = a3f(2 * c.DM + sl * c.DM, c.DM)
            P.emit(V, lambda e, z1t=z1t, xt=xt, yps=yps:
                   e.scalar_tensor_tensor(
                z1t, yps, 1.0, xt, ALU.mult, ALU.add),
                [s_ten.snap(), xw] + zst_w[sl], [(s_vec, 1)])
            psum_war[sl] = [s_vec.snap()]
            st_x.set_war([s_ten.snap()])
            st_ln.set_war([s_vec.snap()])
            zst_w[sl] = [st_z.load(SC, z1_d[b][t * 128:(t + 1) * 128, :],
                                   z1t, waits=[s_vec.snap()])]
        return s_vec.snap()
    def shared_phase(b, ln2_w, entry):
        potst_w = {0: []}
        shT_col = 0
        assert c.NIS * c.S <= AR1W
        POT_B = AR3W - 2 * c.DM
        TMP_COL = POT_B
        assert TMP_COL + c.S <= AR3W and TMP_COL >= 4 * c.ISH
        for isc in range(c.NIS):
            wgb = a3((st_w.k % 2) * c.ISH, c.NK * 128)
            wgw = st_w.load(G, wgb.rearrange("p (k f) -> p k f", f=128),
                            AP(swg_d.tensor, isc * 128,
                               [[c.ISH, 128], [128 * c.ISH, c.NK], [1, 128]]),
                            waits=list(entry) if isc == 0 else [])
            wub = a3(2 * c.ISH + (st_w2.k % 2) * c.ISH, c.NK * 128)
            wuw = st_w2.load(G, wub.rearrange("p (k f) -> p k f", f=128),
                             AP(swu_d.tensor, isc * 128,
                                [[c.ISH, 128], [128 * c.ISH, c.NK],
                                 [1, 128]]),
                             waits=list(entry) if isc == 0 else [])
            gps = PS1[:, 0:min(c.S, 2048)]
            ups = PS2[:, 0:min(c.S, 2048)]
            nstr = cdiv(min(c.S, 2048), 512)
            for k in range(c.NK):
                for ns in range(nstr):
                    n0, n1 = ns * 512, min(c.S, (ns + 1) * 512)
                    P.emit(T, lambda e, k=k, n0=n0, n1=n1, wgb=wgb: e.matmul(
                        gps[:, n0:n1], wgb[:, k * 128:(k + 1) * 128],
                        a2(k * c.S + n0, n1 - n0),
                        start=(k == 0), stop=(k == c.NK - 1)),
                        [wgw, ln2_w, s_scl.snap()] if (k == 0 and ns == 0)
                        else [],
                        [(s_ten, 1)] if (k == c.NK - 1 and ns == nstr - 1)
                        else [])
            g_done = s_ten.snap()
            st_w.set_war(g_done)
            for k in range(c.NK):
                for ns in range(nstr):
                    n0, n1 = ns * 512, min(c.S, (ns + 1) * 512)
                    P.emit(T, lambda e, k=k, n0=n0, n1=n1, wub=wub: e.matmul(
                        ups[:, n0:n1], wub[:, k * 128:(k + 1) * 128],
                        a2(k * c.S + n0, n1 - n0),
                        start=(k == 0), stop=(k == c.NK - 1)),
                        [wuw, s_vec.snap()] if (k == 0 and ns == 0) else [],
                        [(s_ten, 1)] if (k == c.NK - 1 and ns == nstr - 1)
                        else [])
            u_done = s_ten.snap()
            st_w2.set_war(u_done)
            tmp = a3(TMP_COL, c.S)
            P.emit(SC, lambda e, tmp=tmp: e.activation(
                tmp, PS1[:, 0:c.S], AF.Silu),
                [g_done, s_vec.snap()], [(s_scl, 1)])
            P.emit(V, lambda e, tmp=tmp, isc=isc: e.tensor_tensor(
                a1(shT_col + isc * c.S, c.S), tmp, PS2[:, 0:c.S], ALU.mult),
                [u_done, s_scl.snap()] +
                (list(entry) if isc == 0 else []), [(s_vec, 1)])
        sh_done = s_vec.snap()
        n_ar2 = min(c.NIS, AR2W // c.DM)
        assert (c.NIS - n_ar2) * c.DM <= POT_B
        swd_cw = []
        for isc in range(c.NIS):
            dst = (a2(isc * c.DM, c.DM) if isc < n_ar2
                   else a3((isc - n_ar2) * c.DM, c.DM))
            w = st_swd.load(G, dst, AP(swd_d.tensor, isc * 128 * c.DM,
                                       [[c.DM, 128], [1, c.DM]]),
                            waits=[sh_done] if isc < 8 else [])
            st_swd.set_war(w)
            swd_cw.append(w)
        for t in range(c.NT):
            yps = PS1[:, 0:min(c.DM, 2048)]
            war = s_vec.snap()
            nstr = cdiv(min(c.DM, 2048), 512)
            for isc in range(c.NIS):
                rhs = (a2(isc * c.DM, c.DM) if isc < n_ar2
                       else a3((isc - n_ar2) * c.DM, c.DM))
                for ns in range(nstr):
                    n0, n1 = ns * 512, min(c.DM, (ns + 1) * 512)
                    P.emit(T, lambda e, isc=isc, n0=n0, n1=n1, t=t, rhs=rhs:
                           e.matmul(
                        yps[:, n0:n1],
                        a1(shT_col + isc * c.S + t * 128, 128),
                        rhs[:, n0:n1],
                        start=(isc == 0), stop=(isc == c.NIS - 1)),
                        ([sh_done, war] if (isc == 0 and ns == 0) else []) +
                        ([swd_cw[isc]] if (t == 0 and ns == 0) else []),
                        [(s_ten, 1)] if (isc == c.NIS - 1 and
                                         ns == nstr - 1) else [])
            pot = a3(TMP_COL + c.S, c.DM)
            P.emit(V, lambda e, pot=pot: e.tensor_copy(pot, yps),
                   [s_ten.snap()] + potst_w[0], [(s_vec, 1)])
            ng = b * c.NT + t
            potst_w[0] = [st_po.load(
                SC, preout_d[ng * 128:(ng + 1) * 128, :], pot,
                waits=[s_vec.snap()])]
        return s_vec.snap()

    def routing_phase(logits_w):
        l3 = ltok[:].rearrange("p (f e) -> p f e", e=c.E)
        p3 = probs[:].rearrange("p (f e) -> p f e", e=c.E)
        P.emit(V, lambda e: e.reduce_max(gmax[:].unsqueeze(2), l3,
                                         axis=AX.X),
               [logits_w, s_vec.snap()], [(s_vec, 1)])
        P.emit(V, lambda e: e.tensor_sub(
            p3, l3, gmax[:].unsqueeze(2).to_broadcast([128, c.NF, c.E])),
            [s_vec.snap()], [(s_vec, 1)])
        P.emit(SC, lambda e: e.activation(probs[:], probs[:], AF.Exp),
               [s_vec.snap()], [(s_scl, 1)])
        P.emit(V, lambda e: e.reduce_sum(gsum[:].unsqueeze(2), p3,
                                         axis=AX.X),
               [s_scl.snap()], [(s_vec, 1)])
        P.emit(V, lambda e: e.reciprocal(gsum[:], gsum[:]),
               [s_vec.snap()], [(s_vec, 1)])
        P.emit(V, lambda e: e.tensor_tensor(
            p3, p3, gsum[:].unsqueeze(2).to_broadcast([128, c.NF, c.E]),
            ALU.mult), [s_vec.snap()], [(s_vec, 1)])
        for f in range(c.NF):
            P.emit(V, lambda e, f=f: e.max(
                top8[:], probs[:, f * c.E:(f + 1) * c.E]),
                [s_vec.snap()], [(s_vec, 1)])
            P.emit(V, lambda e, f=f: e.max_index(
                idx8[:], top8[:], probs[:, f * c.E:(f + 1) * c.E]),
                [s_vec.snap()], [(s_vec, 1)])
            P.emit(V, lambda e, f=f: e.tensor_copy(
                w12[:].rearrange("p (c f) -> p c f", f=c.NF)[:, :, f],
                top8[:, 0:2]), [s_vec.snap()], [(s_vec, 1)])
            P.emit(V, lambda e, f=f: e.tensor_copy(
                i12f[:].rearrange("p (c f) -> p c f", f=c.NF)[:, :, f],
                idx8[:, 0:2]), [s_vec.snap()], [(s_vec, 1)])
        P.emit(V, lambda e: e.memset(pos12[:], 0.0), [s_vec.snap()],
               [(s_vec, 1)])
        pps = PS1[:, 0:c.NF]
        cps = PS1[0:1, 512:512 + c.NF]
        ctp = PS1[0:c.NF, 1024:1025]
        crp = PS1[0:1, 1536:1536 + c.NF]
        i1v = i12f[:].rearrange("p (c f) -> p c f", f=c.NF)
        p12v = pos12[:].rearrange("p (c f) -> p c f", f=c.NF)
        for ee in range(c.E):
            P.emit(V, lambda e, ee=ee: e.tensor_single_scalar(
                msk1[:], i1v[:, 0, :], float(ee), ALU.is_equal),
                [s_vec.snap()], [(s_vec, 1)])
            P.emit(V, lambda e, ee=ee: e.tensor_single_scalar(
                mske[:], i1v[:, 1, :], float(ee), ALU.is_equal),
                [s_vec.snap()], [(s_vec, 1)])
            P.emit(V, lambda e: e.tensor_add(mske[:], mske[:], msk1[:]),
                   [s_vec.snap()], [(s_vec, 1)])
            mw = s_vec.snap()
            P.emit(T, lambda e: e.matmul(pps, triu[:], mske[:],
                                         start=True, stop=False,
                                         skip_group_check=True),
                   [mw, boot_w, s_vec.snap()], [(s_ten, 1)])
            P.emit(T, lambda e: e.matmul(cps, onesc[:], mske[:],
                                         start=True, stop=True,
                                         skip_group_check=True),
                   [], [(s_ten, 1)])
            P.emit(V, lambda e: e.tensor_copy(csum[:], cps),
                   [s_ten.snap()], [(s_vec, 1)])
            P.emit(T, lambda e: e.transpose(ctp, csum[:], identf[0:1, 0:1]),
                   [s_vec.snap()], [(s_ten, 1)])
            P.emit(V, lambda e: e.tensor_copy(csumT[0:c.NF, :], ctp),
                   [s_ten.snap()], [(s_vec, 1)])
            P.emit(T, lambda e: e.matmul(crp, csumT[0:c.NF, :],
                                         triu[0:c.NF, 0:c.NF],
                                         start=True, stop=True,
                                         skip_group_check=True),
                   [s_vec.snap()], [(s_ten, 1)])
            P.emit(V, lambda e: e.tensor_copy(carrys[:], crp),
                   [s_ten.snap()], [(s_vec, 1)])
            P.emit(T, lambda e: e.matmul(pps, onesr[:], carrys[:],
                                         start=False, stop=True,
                                         skip_group_check=True),
                   [s_vec.snap()], [(s_ten, 1)])
            P.emit(V, lambda e: e.tensor_copy(pose[:], pps),
                   [s_ten.snap()], [(s_vec, 1)])
            for cc in range(2):
                P.emit(V, lambda e, ee=ee, cc=cc: e.tensor_single_scalar(
                    msk1[:], i1v[:, cc, :], float(ee), ALU.is_equal),
                    [s_vec.snap()], [(s_vec, 1)])
                P.emit(V, lambda e: e.tensor_mul(msk1[:], msk1[:], pose[:]),
                       [s_vec.snap()], [(s_vec, 1)])
                P.emit(V, lambda e, cc=cc: e.tensor_add(
                    p12v[:, cc, :], p12v[:, cc, :], msk1[:]),
                    [s_vec.snap()], [(s_vec, 1)])
        P.emit(V, lambda e: e.tensor_single_scalar(
            pos12[:], pos12[:], float(c.CAP - 1), ALU.min),
            [s_vec.snap()], [(s_vec, 1)])
        P.emit(V, lambda e: e.scalar_tensor_tensor(
            scatf[:], i12f[:], float(c.CAP), pos12[:], ALU.mult, ALU.add),
            [s_vec.snap()], [(s_vec, 1)])
        P.emit(V, lambda e: e.tensor_copy(scat[:], scatf[:]),
               [s_vec.snap()], [(s_vec, 1)])
        return s_vec.snap()

    def inv_phase(route_w, entry):
        scv = scat[:].rearrange("p (c f) -> p c f", f=c.NF)
        war_v = {0: [], 1: []}
        for f in range(c.NF):
            b0 = (f % 2) * 2
            P.emit(V, lambda e, f=f, b0=b0: e.tensor_scalar(
                invi[:, b0:b0 + 1], iotas[:, 0:1], float(f * 128), None,
                ALU.add), [boot_w, route_w] + war_v[f % 2] +
                (list(entry) if f < 2 else []), [(s_vec, 1)])
            P.emit(V, lambda e, b0=b0: e.tensor_scalar(
                invi[:, b0 + 1:b0 + 2], invi[:, b0:b0 + 1], float(c.NTOK),
                None, ALU.add), [s_vec.snap()], [(s_vec, 1)])
            P.emit(V, lambda e, b0=b0: e.tensor_copy(
                inviu[:, b0:b0 + 2], invi[:, b0:b0 + 2]),
                [s_vec.snap()], [(s_vec, 1)])
            vw = s_vec.snap()
            for cc in range(2):
                st_sc.emit_load(G, lambda e, f=f, cc=cc, b0=b0:
                                e.indirect_dma_start(
                    ginv_d, IndirectOffsetOnAxis(scv[:, cc, f:f + 1], 0),
                    inviu[:, b0:b0 + 1], None), waits=[vw, route_w])
                st_sc.emit_load(G, lambda e, f=f, cc=cc, b0=b0:
                                e.indirect_dma_start(
                    sinv_d, IndirectOffsetOnAxis(scv[:, cc, f:f + 1], 0),
                    inviu[:, b0 + cc:b0 + cc + 1], None), waits=[])
                st_sc.emit_load(G, lambda e, f=f, cc=cc:
                                e.indirect_dma_start(
                    wslot_d, IndirectOffsetOnAxis(scv[:, cc, f:f + 1], 0),
                    w12[:, cc * c.NF + f:cc * c.NF + f + 1], None),
                    waits=[])
            war_v[f % 2] = st_sc.total()
        return st_sc.total()

    def expert_phase(inv_ws, ln2d_ws, entry):
        eost_w = {0: [], 1: []}
        gv_war = {0: [], 1: []}
        sv_war = {0: [], 1: []}
        XG_COL = c.NI * c.CAP
        EOS_COL = XG_COL + 2 * c.DM
        WD1_COL = EOS_COL + 2 * c.DM
        n_wd1 = min(c.NI, (AR1W - WD1_COL) // c.DM)
        assert 4 * c.DM + c.CAP <= AR3W
        assert c.NK * c.CAP <= AR2W
        assert (c.NI - n_wd1) * c.DM <= AR2W - c.NK * c.CAP

        def wd_ap(isc):
            if isc < n_wd1:
                return a1(WD1_COL + isc * c.DM, c.DM)
            return a2(c.NK * c.CAP + (isc - n_wd1) * c.DM, c.DM)
        prev_down_w = []
        for ee in range(c.E):
            for isc in range(c.NI):
                P.dma(G, wd_ap(isc),
                      AP(wd_d.tensor, ee * c.DI * c.DM + isc * 128 * c.DM,
                         [[c.DM, 128], [1, c.DM]]),
                      s_wdL,
                      waits=(prev_down_w + (list(entry) if ee == 0 else []))
                      if isc == 0 else [])
            wd_w = s_wdL.snap()
            for tt in range(c.NCAP):
                sl = st_xg.k % 2
                xg = a1(XG_COL + sl * c.DM, c.DM)
                r0 = ee * c.CAP + tt * 128
                P.dma(SY, gv[:, sl:sl + 1], ginv_d[r0:r0 + 128, :],
                      s_sydma, waits=gv_war[sl] +
                      ((list(inv_ws) + list(entry))
                       if (ee == 0 and tt == 0) else []))
                gvw = s_sydma.snap()
                xw = st_xg.emit_load(
                    G, lambda e, xg=xg, sl=sl: e.indirect_dma_start(
                        xg, None, ln2t_d,
                        IndirectOffsetOnAxis(gv[:, sl:sl + 1], 0)),
                    waits=[gvw] + ((list(ln2d_ws) + list(entry))
                                   if (ee == 0 and tt == 0) else []))
                gv_war[sl] = [xw]
                evw = transpose_tile(
                    lambda k, xg=xg: xg[:, k * 128:(k + 1) * 128],
                    AR2, 0, c.CAP, tt * 128, c.NK, [xw] + (
                        list(entry) if (ee == 0 and tt == 0) else []))
                st_xg.set_war(s_ten.snap())
            xgT_w = s_vec.snap()
            for isc in range(c.NI):
                wgb = a3((st_w.k % 2) * c.DM, c.DM)
                wgw = st_w.load(G, wgb.rearrange("p (k f) -> p k f", f=128),
                                AP(wg_d.tensor, ee * c.DM * c.DI + isc * 128,
                                   [[c.DI, 128], [128 * c.DI, c.NK],
                                    [1, 128]]),
                                waits=list(entry) if (ee == 0 and isc == 0)
                                else [])
                wub = a3(2 * c.DM + (st_w2.k % 2) * c.DM, c.DM)
                wuw = st_w2.load(G, wub.rearrange("p (k f) -> p k f", f=128),
                                 AP(wu_d.tensor, ee * c.DM * c.DI +
                                    isc * 128,
                                    [[c.DI, 128], [128 * c.DI, c.NK],
                                     [1, 128]]),
                                 waits=list(entry) if (ee == 0 and isc == 0)
                                 else [])
                gps = PS1[:, 0:min(c.CAP, 2048)]
                ups = PS2[:, 0:min(c.CAP, 1536)]
                assert c.CAP <= 1536
                nstr = cdiv(c.CAP, 512)
                for k in range(c.NK):
                    for ns in range(nstr):
                        n0, n1 = ns * 512, min(c.CAP, (ns + 1) * 512)
                        P.emit(T, lambda e, k=k, n0=n0, n1=n1, wgb=wgb:
                               e.matmul(
                            gps[:, n0:n1], wgb[:, k * 128:(k + 1) * 128],
                            a2(k * c.CAP + n0, n1 - n0),
                            start=(k == 0), stop=(k == c.NK - 1)),
                            [wgw, xgT_w, s_scl.snap()]
                            if (k == 0 and ns == 0) else [],
                            [(s_ten, 1)] if (k == c.NK - 1 and
                                             ns == nstr - 1) else [])
                g_done = s_ten.snap()
                st_w.set_war(g_done)
                for k in range(c.NK):
                    for ns in range(nstr):
                        n0, n1 = ns * 512, min(c.CAP, (ns + 1) * 512)
                        P.emit(T, lambda e, k=k, n0=n0, n1=n1, wub=wub:
                               e.matmul(
                            ups[:, n0:n1], wub[:, k * 128:(k + 1) * 128],
                            a2(k * c.CAP + n0, n1 - n0),
                            start=(k == 0), stop=(k == c.NK - 1)),
                            [wuw, s_vec.snap()] if (k == 0 and ns == 0)
                            else [],
                            [(s_ten, 1)] if (k == c.NK - 1 and
                                             ns == nstr - 1) else [])
                u_done = s_ten.snap()
                st_w2.set_war(u_done)
                tmp = a3(4 * c.DM, c.CAP)
                P.emit(SC, lambda e, tmp=tmp: e.activation(
                    tmp, PS1[:, 0:c.CAP], AF.Silu),
                    [g_done, s_vec.snap()], [(s_scl, 1)])
                P.emit(V, lambda e, tmp=tmp, isc=isc: e.tensor_tensor(
                    a1(isc * c.CAP, c.CAP), tmp, PS2[:, 0:c.CAP], ALU.mult),
                    [u_done, s_scl.snap()] +
                    (list(entry) if (ee == 0 and isc == 0) else []),
                    [(s_vec, 1)])
            h_done = s_vec.snap()
            for tt in range(c.NCAP):
                yps = PS1[:, 0:min(c.DM, 2048)]
                war = s_vec.snap()
                nstr = cdiv(min(c.DM, 2048), 512)
                for isc in range(c.NI):
                    for ns in range(nstr):
                        n0, n1 = ns * 512, min(c.DM, (ns + 1) * 512)
                        P.emit(T, lambda e, isc=isc, n0=n0, n1=n1, tt=tt:
                               e.matmul(
                            yps[:, n0:n1],
                            a1(isc * c.CAP + tt * 128, 128),
                            wd_ap(isc)[:, n0:n1],
                            start=(isc == 0), stop=(isc == c.NI - 1)),
                            [wd_w, h_done, war] if (isc == 0 and ns == 0)
                            else [],
                            [(s_ten, 1)] if (isc == c.NI - 1 and
                                             ns == nstr - 1) else [])
                sl = st_eo.k % 2
                eos = a1(EOS_COL + sl * c.DM, c.DM)
                r0 = ee * c.CAP + tt * 128
                P.dma(SY, gv[:, 2 + sl:3 + sl], sinv_d[r0:r0 + 128, :],
                      s_sydma, waits=sv_war[sl] +
                      (list(inv_ws) if (ee == 0 and tt == 0) else []))
                P.dma(SY, wsl[:, sl:sl + 1], wslot_d[r0:r0 + 128, :],
                      s_sydma)
                svw = s_sydma.snap()
                P.emit(V, lambda e, eos=eos, sl=sl: e.tensor_scalar(
                    eos, yps, wsl[:, sl:sl + 1], None, ALU.mult),
                    [s_ten.snap(), svw] + eost_w[sl], [(s_vec, 1)])
                scw = st_eo.emit_load(
                    G, lambda e, eos=eos, sl=sl: e.indirect_dma_start(
                        contrib_d,
                        IndirectOffsetOnAxis(gv[:, 2 + sl:3 + sl], 0),
                        eos, None),
                    waits=[s_vec.snap()] +
                    ([expin_zero_w] if (ee == 0 and tt == 0) else []))
                eost_w[sl] = [scw]
                sv_war[sl] = [scw]
            prev_down_w = [s_ten.snap()]
        return st_eo.total()

    def final_phase(co_ws, preout_ws, z1_ws, entry):
        otst_w = {0: [], 1: []}
        for f in range(c.NF):
            b, t = f // c.NT, f % c.NT
            pot = a3(8192 + (f % 2) * c.DM, c.DM)
            pw = st_x.load(SY, pot, preout_d[f * 128:(f + 1) * 128, :],
                           waits=(list(preout_ws) + list(entry))
                           if f == 0 else [])
            z1t = a3f((f % 2) * c.DM, c.DM)
            zw = st_ln.load(SY, z1t, z1_d[b][t * 128:(t + 1) * 128, :],
                            waits=list(z1_ws) if f == 0 else [])
            g1 = a1((st_ga.k % 2) * c.DM, c.DM)
            gw1 = st_ga.load(G, g1, contrib_d[f * 128:(f + 1) * 128, :],
                             waits=(list(co_ws) + list(entry))
                             if f == 0 else [])
            g2 = a1(2 * c.DM + (st_gb.k % 2) * c.DM, c.DM)
            gw2 = st_gb.load(G, g2,
                             contrib_d[c.NTOK + f * 128:
                                       c.NTOK + (f + 1) * 128, :],
                             waits=list(co_ws) if f == 0 else [])
            ot = a2f((f % 2) * c.DM, c.DM)
            P.emit(V, lambda e, z1t=z1t, pot=pot, ot=ot: e.tensor_tensor(
                ot, z1t, pot, ALU.add),
                [zw, pw, s_vec.snap()] + otst_w[f % 2] +
                (list(entry) if f < 2 else []),
                [(s_vec, 1)])
            st_x.set_war(s_vec.snap())
            st_ln.set_war(s_vec.snap())
            P.emit(V, lambda e, g1=g1, ot=ot: e.tensor_add(ot, ot, g1),
                   [gw1, s_vec.snap()], [(s_vec, 1)])
            st_ga.set_war(s_vec.snap())
            P.emit(V, lambda e, g2=g2, ot=ot: e.tensor_add(ot, ot, g2),
                   [gw2, s_vec.snap()], [(s_vec, 1)])
            st_gb.set_war(s_vec.snap())
            otst_w[f % 2] = [st_out.load(
                SY, out_d[b][t * 128:(t + 1) * 128, :], ot,
                waits=[s_vec.snap()])]
        P.emit(SY, lambda e: None, st_out.total())

    for b in range(c.NB):
        lnw = ln_phase(b, "ln1",
                       [] if b == 0 else P.barrier() + st_z.total())
        store_ws, piece_ws = attn_batch(
            b, lnw, P.barrier() +
            ([explb_w, expin_zero_w] if b == 0 else []))
        wo_phase(b, store_ws, piece_ws)
    z1_ws = st_z.total()
    for b in range(c.NB):
        ln2w = ln_phase(b, "ln2", P.barrier() + st_z.total() +
                        st_po.total())
        shared_phase(b, ln2w, P.barrier() + st_lnst.total())
    preout_ws = st_po.total()
    logits_w = s_vec.snap()
    ln2d_ws = st_lnst.total()
    route_w = routing_phase(logits_w)
    inv_ws = inv_phase(route_w, P.barrier())
    co_ws = expert_phase(inv_ws, ln2d_ws, P.barrier())
    final_phase(co_ws, preout_ws, z1_ws, P.barrier())

    with nc.Block() as block:
        P.finalize(block)
    return nc


def host_constants(c: Cfg):
    import ml_dtypes
    half = 32
    inv = 1.0 / (10000.0 ** (np.arange(half, dtype=np.float64) / half))
    t = np.arange(c.S, dtype=np.float64)
    ang = np.outer(inv, t)
    cos, sin = np.cos(ang), np.sin(ang)
    C = np.tile(cos, (4, 1)).astype(np.float32)
    Srow = np.concatenate([-sin, sin, -sin, sin], 0).astype(np.float32)
    return {
        "cst": np.full((128, 1), 1e-5, np.float32),
        "identb": np.eye(128, dtype=np.float32).astype(ml_dtypes.bfloat16),
        "identf": np.eye(128, dtype=np.float32),
        "ropeC": np.ascontiguousarray(C),
        "ropeS": np.ascontiguousarray(Srow),
        "triu": np.triu(np.ones((128, 128), np.float32), 1),
        "onesr": np.ones((1, 128), np.float32),
        "onesc": np.ones((128, 1), np.float32),
        "dgidx": (128 - np.arange(128, dtype=np.uint32)).reshape(128, 1),
        "iota": np.arange(128, dtype=np.float32).reshape(128, 1),
    }


def host_inputs(c: Cfg, core, inputs):
    cc = np.ascontiguousarray
    x = inputs["L0_attn_hidden_states"]
    m = {
        "x": cc(x[core * c.NB:(core + 1) * c.NB]),
        "wqT": cc(np.asarray(inputs["L0_attn_Wq"]).T),
        "wkT": cc(np.asarray(inputs["L0_attn_Wk"]).T),
        "wvT": cc(np.asarray(inputs["L0_attn_Wv"]).T),
        "woT": cc(np.asarray(inputs["L0_attn_Wo"]).T),
        "lb": cc(np.asarray(inputs["L0_attn_learnable_bias"])),
        "soff": cc(np.asarray(inputs["L0_attn_softmax_offset"])
                   .reshape(1, c.H)),
        "gwT": cc(np.asarray(inputs["L0_ffn_gate_w"]).T),
        "wg": cc(np.asarray(inputs["L0_ffn_w_gate"])),
        "wu": cc(np.asarray(inputs["L0_ffn_w_up"])),
        "wd": cc(np.asarray(inputs["L0_ffn_w_down"])),
        "swg": cc(np.asarray(inputs["L0_ffn_sw_gate"])),
        "swu": cc(np.asarray(inputs["L0_ffn_sw_up"])),
        "swd": cc(np.asarray(inputs["L0_ffn_sw_down"])),
    }
    m.update(host_constants(c))
    return m




_CFG = Cfg()
_N_CORES = 8


def kernel(**inputs):
    import contextlib
    from concourse.bass_utils import run_bass_kernel_spmd
    c = _CFG
    nc = bass.Bass("TRN2", target_bir_lowering=False)
    ctx = contextlib.ExitStack()
    build_program(c, nc, ctx)
    ctx.close()
    nc.finalize()
    in_maps = [host_inputs(c, core, inputs) for core in range(_N_CORES)]
    res = run_bass_kernel_spmd(nc, in_maps, core_ids=list(range(_N_CORES)))
    outs = [np.asarray(r["out"]).reshape(c.NB, c.S, c.DM).astype(np.float32)
            for r in res.results]
    return np.concatenate(outs, axis=0)


# version 14
# speedup vs baseline: 1.0558x; 1.0549x over previous
import numpy as np
import concourse.bass as bass
import concourse.mybir as mybir
from concourse.ap import AP
from concourse.bass import IndirectOffsetOnAxis

F32 = mybir.dt.float32
BF16 = mybir.dt.bfloat16
U32 = mybir.dt.uint32
AF = mybir.ActivationFunctionType
ALU = mybir.AluOpType
AX = mybir.AxisListType

ENGS = ("sync", "tensor", "vector", "scalar", "gpsimd")
SY, T, V, SC, G = ENGS


class Cfg:
    def __init__(self, S=2048, DM=2048, NB=2, DI=1408, ISH=2816, CAP=1280,
                 W=1024, E=8):
        self.S, self.DM, self.NB = S, DM, NB
        self.H = DM // 64
        self.E, self.DI, self.ISH, self.CAP, self.W = E, DI, ISH, CAP, W
        self.NT = S // 128
        self.NK = DM // 128
        self.NTOK = NB * S
        self.NF = self.NTOK // 128
        self.NIG = S // 512
        self.C_EB = 896 + min(W, max(0, S - 512))
        self.PADN = 512
        self.NI = DI // 128
        self.NIS = ISH // 128
        self.NCAP = CAP // 128
        assert S % 512 == 0 and DM % 128 == 0 and DI % 128 == 0
        assert ISH % 128 == 0 and CAP % 128 == 0 and W % 128 == 0
        assert self.NF <= 128 and self.E == 8 and self.NK % 4 == 0


class Sem:
    __slots__ = ("h", "n", "name")

    def __init__(self, h, name):
        self.h, self.n, self.name = h, 0, name

    def snap(self):
        return (self, self.n)


class Prog:
    def __init__(self, nc, ctx):
        self.nc, self.ctx = nc, ctx
        self.stream = {e: [] for e in ENGS}

    def sem(self, name):
        return Sem(self.ctx.enter_context(self.nc.semaphore(name)), name)

    def emit(self, eng, fn, waits=(), incs=()):
        ws = [(s, int(v)) for (s, v) in waits if s is not None]
        ics = []
        for (s, k) in incs:
            s.n += k
            ics.append((s, k))
        self.stream[eng].append((fn, ws, ics))

    def dma(self, q, out, in_, sem, waits=(), **kw):
        self.emit(q, lambda e, o=out, i=in_, kw=kw: e.dma_start(o, i, **kw),
                  waits, [(sem, 16)])

    def barrier(self):
        return [s.snap() for s in (self.s_ten, self.s_vec, self.s_scl,
                                   self.s_gpe)]

    def finalize(self, block):
        for eng in ENGS:
            ops = self.stream[eng]
            if not ops:
                continue

            def body(e, ops=ops):
                for fn, ws, ics in ops:
                    for s, v in ws:
                        e.wait_ge(s.h, v)
                    inst = fn(e)
                    if inst is None:
                        continue
                    for s, k in ics:
                        inst.then_inc(s.h, k)
            getattr(block, eng)(body)


class Stream:

    def __init__(self, P, name, nslots=2):
        self.P, self.nslots = P, nslots
        self.ld = [P.sem(f"{name}_l{i}") for i in range(nslots)]
        self.k = 0
        self.war = [[] for _ in range(nslots)]

    def load(self, q, out, in_, waits=(), **kw):
        i = self.k % self.nslots
        ws = list(waits) + list(self.war[i])
        self.P.dma(q, out, in_, self.ld[i], waits=ws, **kw)
        self.k += 1
        return (self.ld[i], 16 * ((self.k - 1) // self.nslots + 1))

    def emit_load(self, eng, fn, waits=()):
        i = self.k % self.nslots
        ws = list(waits) + list(self.war[i])
        self.P.emit(eng, fn, ws, [(self.ld[i], 16)])
        self.k += 1
        return (self.ld[i], 16 * ((self.k - 1) // self.nslots + 1))

    def set_war(self, snap):
        w = snap if isinstance(snap, list) else [snap]
        self.war[(self.k - 1) % self.nslots] = list(w)

    def total(self):
        return [(self.ld[i], 16 * ((self.k - 1 - i) // self.nslots + 1))
                for i in range(min(self.k, self.nslots))]


def cdiv(a, b):
    return (a + b - 1) // b


def build_program(c: Cfg, nc, ctx):
    P = Prog(nc, ctx)
    dt = nc.dram_tensor
    AR1W, AR2W, AR3W = 45184, c.NK * c.S, 16384

    x_d = dt("x", [c.NB, c.S, c.DM], F32, kind="ExternalInput").ap()
    wqT_d = dt("wqT", [c.DM, c.DM], F32, kind="ExternalInput").ap()
    wkT_d = dt("wkT", [c.DM, c.DM], F32, kind="ExternalInput").ap()
    wvT_d = dt("wvT", [c.DM, c.DM], F32, kind="ExternalInput").ap()
    woT_d = dt("woT", [c.DM, c.DM], F32, kind="ExternalInput").ap()
    lb_d = dt("lb", [c.H, c.W], F32, kind="ExternalInput").ap()
    soff_d = dt("soff", [1, c.H], F32, kind="ExternalInput").ap()
    gwT_d = dt("gwT", [c.DM, c.E], F32, kind="ExternalInput").ap()
    wg_d = dt("wg", [c.E, c.DM, c.DI], F32, kind="ExternalInput").ap()
    wu_d = dt("wu", [c.E, c.DM, c.DI], F32, kind="ExternalInput").ap()
    wd_d = dt("wd", [c.E, c.DI, c.DM], F32, kind="ExternalInput").ap()
    swg_d = dt("swg", [c.DM, c.ISH], F32, kind="ExternalInput").ap()
    swu_d = dt("swu", [c.DM, c.ISH], F32, kind="ExternalInput").ap()
    swd_d = dt("swd", [c.ISH, c.DM], F32, kind="ExternalInput").ap()
    cst_d = dt("cst", [128, 1], F32, kind="ExternalInput").ap()
    identb_d = dt("identb", [128, 128], BF16, kind="ExternalInput").ap()
    identf_d = dt("identf", [128, 128], F32, kind="ExternalInput").ap()
    ropeC_d = dt("ropeC", [128, c.S], F32, kind="ExternalInput").ap()
    ropeS_d = dt("ropeS", [128, c.S], F32, kind="ExternalInput").ap()
    triu_d = dt("triu", [128, 128], F32, kind="ExternalInput").ap()
    onesr_d = dt("onesr", [1, 128], F32, kind="ExternalInput").ap()
    onesc_d = dt("onesc", [128, 1], F32, kind="ExternalInput").ap()
    dgidx_d = dt("dgidx", [128, 1], U32, kind="ExternalInput").ap()
    iota_d = dt("iota", [128, 1], F32, kind="ExternalInput").ap()
    out_d = dt("out", [c.NB, c.S, c.DM], F32, kind="ExternalOutput").ap()

    z1_d = dt("z1s", [c.NB, c.S, c.DM], F32, kind="Internal").ap()
    attnT_d = dt("attnTs", [c.NB, c.NT, 128, c.NK * 128], BF16,
                 kind="Internal").ap()
    explb_d = dt("explbs", [c.H, c.PADN + c.S], F32, kind="Internal").ap()
    ln2t_d = dt("ln2ts", [c.NTOK, c.DM], BF16, kind="Internal").ap()
    preout_d = dt("preouts", [c.NTOK, c.DM], BF16, kind="Internal").ap()
    inv_d = dt("invs", [c.E * c.CAP, 4], U32, kind="Internal").ap()
    contrib_d = dt("contribs", [2 * c.NTOK + 128, c.DM], BF16,
                   kind="Internal").ap()

    sbt = lambda n, sh, d: ctx.enter_context(nc.sbuf_tensor(n, sh, d))
    AR1 = sbt("AR1", [128, AR1W], BF16)
    AR2 = sbt("AR2", [128, AR2W], BF16)
    AR3 = sbt("AR3", [128, AR3W], BF16)

    def a1(col, n):
        return AR1[:, col:col + n]

    def a2(col, n):
        return AR2[:, col:col + n]

    def a3(col, n):
        return AR3[:, col:col + n]

    def a1f(col, n):
        return AR1[:, 2 * col:2 * (col + n)].bitcast(F32)

    def a2f(col, n):
        return AR2[:, 2 * col:2 * (col + n)].bitcast(F32)

    def a3f(col, n):
        return AR3[:, 2 * col:2 * (col + n)].bitcast(F32)

    cstv = sbt("cstvs", [128, 1], F32)
    identb = sbt("identbs", [128, 128], BF16)
    identf = sbt("identfs", [128, 128], F32)
    triu = sbt("trius", [128, 128], F32)
    onesr = sbt("onesrs", [1, 128], F32)
    onesc = sbt("onescs", [128, 1], F32)
    dgidx = sbt("dgidxs", [128, 1], U32)
    iotas = sbt("iotass", [128, 1], F32)
    invi = sbt("invis", [128, 320], F32)
    inviu = sbt("invius", [128, 320], U32)
    gv = sbt("gvs", [128, 40], U32)
    gwT = sbt("gwTs", [128, c.NK * c.E], BF16)
    soff64 = sbt("soff64s", [128, 2 * c.H], F32)
    ropeC = sbt("ropeCs", [128, c.S], BF16)
    ropeS = sbt("ropeSs", [128, c.S], BF16)
    ltok = sbt("ltoks", [128, c.NF * c.E], F32)
    gmax = sbt("gmaxs", [128, c.NF], F32)
    gsum = sbt("gsums", [128, c.NF], F32)
    probs = sbt("probss", [128, c.NF * c.E], F32)
    top8 = sbt("top8s", [128, 8], F32)
    idx8 = sbt("idx8s", [128, 8], U32)
    i12f = sbt("i12fs", [128, 2 * c.NF], F32)
    w12 = sbt("w12s", [128, 2 * c.NF], F32)
    pos12 = sbt("pos12s", [128, 2 * c.NF], F32)
    scatf = sbt("scatfs", [128, 2 * c.NF], F32)
    scat = sbt("scats", [128, 2 * c.NF], U32)
    mske = sbt("mskes", [128, c.NF], F32)
    msk1 = sbt("msk1s", [128, c.NF], F32)
    pose = sbt("poses", [128, c.NF], F32)
    csum = sbt("csums", [1, c.NF], F32)
    csumT = sbt("csumTs", [128, 1], F32)
    carrys = sbt("carryss", [1, c.NF], F32)
    stats = sbt("statss", [128, 8], F32)
    statb = sbt("statbs", [128, 8], F32)

    PS1 = ctx.enter_context(nc.psum_tensor("PS1", [128, 2048], F32))
    PS2 = ctx.enter_context(nc.psum_tensor("PS2", [128, 2048], F32))
    PS2W = 2048
    trps = PS2[:, 1536:2048].bitcast(BF16)

    s_vec = P.sem("vec")
    s_scl = P.sem("scl")
    s_ten = P.sem("ten")
    s_gpe = P.sem("gpe")
    s_sydma = P.sem("sydma")
    s_gpdma = P.sem("gpdma")
    s_exl = P.sem("exl")
    s_zr = P.sem("zr")
    s_woL = P.sem("woL")
    s_swdL = P.sem("swdL")
    s_wdL = P.sem("wdL")
    s_boot = P.sem("boot")
    P.s_vec, P.s_scl, P.s_ten, P.s_gpe = s_vec, s_scl, s_ten, s_gpe

    st_x = Stream(P, "x")
    st_w = Stream(P, "w")
    st_w2 = Stream(P, "w2")
    st_wv = Stream(P, "wv", 1)
    st_at = Stream(P, "at", 2)
    st_swd = Stream(P, "swd", 8)
    st_eb = Stream(P, "eb")
    st_ln = Stream(P, "ln")
    st_lnx = Stream(P, "lnx")
    st_ga = Stream(P, "ga")
    st_gb = Stream(P, "gb")
    st_xg = Stream(P, "xg")
    st_z = Stream(P, "z")
    st_lnst = Stream(P, "lnst")
    st_po = Stream(P, "po", 1)
    st_eo = Stream(P, "eo")
    st_sc = Stream(P, "sc")
    st_out = Stream(P, "outs")

    nc.const_aps.register(F32, 1e-5, cstv[:, 0:1])
    for dst, src in ((cstv[:], cst_d), (identb[:], identb_d),
                     (identf[:], identf_d),
                     (triu[:], triu_d), (onesr[:], onesr_d),
                     (onesc[:], onesc_d), (dgidx[:], dgidx_d),
                     (iotas[:], iota_d), (ropeC[:], ropeC_d),
                     (ropeS[:], ropeS_d), (soff64[64:65, 0:c.H], soff_d)):
        P.dma(G, dst, src, s_boot)
    P.dma(G, gwT[:].rearrange("p (k e) -> p k e", e=c.E),
          AP(gwT_d.tensor, 0, [[c.E, 128], [128 * c.E, c.NK], [1, c.E]]),
          s_boot)
    boot_w = s_boot.snap()

    LEX = c.PADN + c.S
    exl = a1f(2048, LEX)[0:c.H, :]
    exl2 = a1f(2048 + LEX, LEX)[0:c.H, :]
    P.emit(G, lambda e: e.memset(exl[:, 0:c.PADN], -1e30), [], [(s_gpe, 1)])
    P.emit(G, lambda e: e.memset(exl[:, c.PADN + c.W:LEX], 0.0), [],
           [(s_gpe, 1)])
    P.dma(G, exl[:, c.PADN:c.PADN + c.W], lb_d, s_exl, waits=[s_gpe.snap()])
    P.emit(SC, lambda e: e.activation(exl2, exl, AF.Exp),
           [s_exl.snap(), s_gpe.snap()], [(s_scl, 1)])
    P.dma(G, explb_d, exl2, s_exl, waits=[s_scl.snap()])
    explb_w = s_exl.snap()

    P.emit(SC, lambda e: e.activation(soff64[64:65, c.H:2 * c.H],
                                      soff64[64:65, 0:c.H], AF.Exp),
           [boot_w], [(s_scl, 1)])
    eoff_w = s_scl.snap()

    P.emit(G, lambda e: e.memset(invi[:], 0.0), [], [(s_gpe, 1)])
    P.emit(V, lambda e: e.tensor_scalar(
        invi[:].rearrange("p (f q) -> p f q", q=4)[:, :, 1],
        iotas[:, 0:1].to_broadcast([128, 80]),
        float(2 * c.NTOK), None, ALU.add),
        [boot_w, s_gpe.snap()], [(s_vec, 1)])
    P.emit(V, lambda e: e.tensor_copy(inviu[:], invi[:]),
           [s_vec.snap()], [(s_vec, 1)])
    P.dma(G, AP(inv_d.tensor, 0, [[320, 128], [1, 320]]), inviu[:],
          s_zr, waits=[s_vec.snap()])
    ZW = 4096
    zt = a1(0, ZW)
    P.emit(G, lambda e: e.memset(zt, 0.0), [], [(s_gpe, 1)])
    ztot = (2 * c.NTOK + 128) * c.DM
    assert ztot % ZW == 0
    zoff = 0
    while zoff < ztot:
        n = min(128 * ZW, ztot - zoff)
        npart = n // ZW
        P.dma(G, AP(contrib_d.tensor, zoff, [[ZW, npart], [1, ZW]]),
              zt[0:npart, :], s_zr, waits=[s_gpe.snap()])
        zoff += n
    expin_zero_w = s_zr.snap()

    def transpose_tile(src_sb_fn, dst_t, dst_col0, dst_stride, t0, nk,
                       first_waits):
        for g0 in range(0, nk, 8):
            gn = min(8, nk - g0)
            for j in range(gn):
                k = g0 + j
                ws = [s_vec.snap()] if j == 0 else []
                if g0 == 0 and j == 0:
                    ws += list(first_waits)
                P.emit(T, lambda e, k=k, j=j: e.transpose(
                    trps[:, j * 128:(j + 1) * 128], src_sb_fn(k), identb[:]),
                    ws, [(s_ten, 1)] if j == gn - 1 else [])

            def ev(e, g0=g0, gn=gn):
                dst = AP(dst_t, dst_col0 + g0 * dst_stride + t0,
                         [[int(dst_t.shape[1]), 128],
                          [dst_stride, gn], [1, 128]])
                src = trps[:, 0:gn * 128].rearrange("p (g f) -> p g f", g=gn)
                return e.tensor_copy(dst, src)
            P.emit(V, ev, [s_ten.snap()], [(s_vec, 1)])
        return s_vec.snap()

    def ln_phase(b, kind, entry):
        evw = None
        lnst_w = {0: [], 1: []}
        src = x_d[b] if kind == "ln1" else z1_d[b]
        lane_w = {0: [], 1: []}
        sq_w = {0: [], 1: []}
        for t in range(c.NT):
            sl = t % 2
            S0, S1, S2, S3 = 4 * sl, 4 * sl + 1, 4 * sl + 2, 4 * sl + 3
            B0, B1, B2 = 2 * sl, 2 * sl + 1, 4 + sl
            xt = a3f(3 * c.DM // 2 + sl * c.DM, c.DM)
            lnt = a3(sl * c.DM, c.DM)
            xw = st_lnx.load(SY, xt, src[t * 128:(t + 1) * 128, :],
                           waits=list(entry) if t < 2 else [])
            P.emit(V, lambda e, xt=xt, S0=S0: e.reduce_sum(
                stats[:, S0:S0 + 1], xt, axis=AX.X),
                [xw] + lane_w[sl] + sq_w[sl], [(s_vec, 1)])
            sum_w = s_vec.snap()
            P.emit(SC, lambda e, xt=xt, S1=S1: e.activation(
                a3(2 * c.DM, c.DM), xt,
                AF.Square, accum_out=stats[:, S1:S1 + 1]),
                [xw] + lane_w[sl] + sq_w[0] + sq_w[1], [(s_scl, 1)])
            sq_done = s_scl.snap()
            P.emit(V, lambda e, S0=S0, S2=S2: e.tensor_mul(
                stats[:, S2:S2 + 1], stats[:, S0:S0 + 1],
                stats[:, S0:S0 + 1]), [sum_w, s_vec.snap()], [(s_vec, 1)])
            P.emit(V, lambda e, S1=S1, S2=S2, S3=S3: e.scalar_tensor_tensor(
                stats[:, S3:S3 + 1], stats[:, S2:S2 + 1], -1.0 / c.DM,
                stats[:, S1:S1 + 1], ALU.mult, ALU.add),
                [s_vec.snap(), sq_done], [(s_vec, 1)])
            P.emit(SC, lambda e, S3=S3, B0=B0: e.activation(
                statb[:, B0:B0 + 1], stats[:, S3:S3 + 1], AF.Sqrt, bias=1e-5,
                scale=1.0 / c.DM), [s_vec.snap(), boot_w], [(s_scl, 1)])
            P.emit(V, lambda e, B0=B0, B1=B1: e.reciprocal(
                statb[:, B1:B1 + 1], statb[:, B0:B0 + 1]),
                [s_scl.snap()], [(s_vec, 1)])
            P.emit(V, lambda e, S0=S0, B2=B2: e.tensor_scalar(
                statb[:, B2:B2 + 1], stats[:, S0:S0 + 1], -1.0 / c.DM, None,
                ALU.mult), [sum_w, s_vec.snap()], [(s_vec, 1)])
            P.emit(V, lambda e, xt=xt, lnt=lnt, B1=B1, B2=B2: e.tensor_scalar(
                lnt, xt, statb[:, B2:B2 + 1], statb[:, B1:B1 + 1],
                ALU.add, ALU.mult),
                [s_vec.snap()] + lnst_w[sl] + (list(entry) if t < 2 else []),
                [(s_vec, 1)])
            lane_w[sl] = [s_vec.snap()]
            sq_w[sl] = [s_scl.snap()]
            st_lnx.set_war(s_vec.snap())
            ln_w = s_vec.snap()
            if kind == "ln2":
                ng = b * c.NT + t
                lnst_w[sl] = [st_lnst.load(
                    SC, ln2t_d[ng * 128:(ng + 1) * 128, :], lnt,
                    waits=[ln_w])]
            evw = transpose_tile(
                lambda k, lnt=lnt: lnt[:, k * 128:(k + 1) * 128],
                AR2, 0, c.S, t * 128, c.NK,
                [ln_w] + (list(entry) if t == 0 else []))
            if kind == "ln2":
                pass
            if kind == "ln2":
                ng = b * c.NT + t
                lps = PS1[0:c.E, 256:384]
                for k in range(c.NK):
                    P.emit(T, lambda e, k=k, t=t: e.matmul(
                        lps[:, 0:128], gwT[:, k * c.E:(k + 1) * c.E],
                        a2(k * c.S + t * 128, 128),
                        start=(k == 0), stop=(k == c.NK - 1),
                        skip_group_check=True),
                        [evw, boot_w, s_vec.snap()] if k == 0 else [],
                        [(s_ten, 1)] if k == c.NK - 1 else [])
                lsb = a3f(7 * c.DM // 2, 128)[0:c.E, :]
                P.emit(V, lambda e, lsb=lsb: e.tensor_copy(lsb, lps[:, 0:128]),
                       [s_ten.snap()], [(s_vec, 1)])
                tps = PS2[:, 1280:1280 + c.E]
                P.emit(T, lambda e, lsb=lsb, tps=tps: e.transpose(
                    tps, lsb, identf[0:c.E, 0:c.E]),
                    [s_vec.snap()], [(s_ten, 1)])
                P.emit(V, lambda e, ng=ng, tps=tps: e.tensor_copy(
                    ltok[:, ng * c.E:(ng + 1) * c.E], tps),
                    [s_ten.snap()], [(s_vec, 1)])
        return s_vec.snap()

    PB_COL = 0
    NPSL = 6
    VB_COL = NPSL * 2048
    KB_COL = VB_COL + c.NT * 260
    QB_COL = KB_COL + c.S
    QSB_COL = QB_COL + c.S
    QSW_COL = QSB_COL + c.S
    DNM_F = (QSW_COL + c.S) // 2
    WV_COL = QSW_COL + c.S + 8192
    assert WV_COL + c.NK * 256 <= AR1W
    EB_COL = 4096
    EXPS_COL = EB_COL + 2 * c.C_EB
    STG_COL = EXPS_COL + 4 * 512
    RSCR_F = (STG_COL + 2 * c.DM) // 2
    assert 2 * RSCR_F + 2048 <= AR3W
    BAND = c.W + 128
    WQK_COL = 0

    def attn_batch(b, lnxT_w, entry):
        dr = a1f(DNM_F, 4096)

        def dnm(s):
            return dr[64:65, s * 2048:(s + 1) * 2048]

        def rec(s, col0=0, n=2048):
            return dr[0:1, s * 2048 + col0:s * 2048 + col0 + n]

        def rtile(s):
            return dr[64:128, s * 2048:s * 2048 + 32]

        ent = list(entry)
        slot_war = [ent + [s_vec.snap()] for _ in range(4)]
        exps_war = [ent + [s_vec.snap()] for _ in range(4)]
        pslot_war = [ent + [s_ten.snap()] for _ in range(NPSL)]
        bp_war = {q: ent + [s_vec.snap()] for q in range(4)}
        stage_war = {0: list(ent), 1: list(ent)}
        dnm_war = {0: list(ent), 1: list(ent)}
        qsb_war = [[s_vec.snap()]]
        qsw_war = [[s_vec.snap()]]
        piece_ready = {}
        bank_done = {}
        evac_done = {}
        mult_done = {}
        recip_done = {}
        vready = [None]
        qk_w = [None]
        ps1_i = [0]
        exps_i = [0]
        woL_ws = []

        def ps1_slot():
            i = ps1_i[0] % 4
            ps1_i[0] += 1
            return i

        def vproj(sb):
            wv = a1(WV_COL, c.NK * 256)
            wvw = st_wv.load(G, wv.rearrange("p (k f) -> p k f", f=256),
                             AP(wvT_d.tensor, sb * 256,
                                [[c.DM, 128], [128 * c.DM, c.NK], [1, 256]]),
                             waits=(ent + [lnxT_w]) if sb == 0 else [])
            for t in range(c.NT):
                si = ps1_slot()
                sps = PS1[:, si * 512:si * 512 + 256]
                for k in range(c.NK):
                    P.emit(T, lambda e, k=k, t=t, sps=sps, wv=wv: e.matmul(
                        sps, a2(k * c.S + t * 128, 128),
                        wv[:, k * 256:(k + 1) * 256],
                        start=(k == 0), stop=(k == c.NK - 1),
                        skip_group_check=True),
                        ([wvw] + slot_war[si] +
                         ([lnxT_w] if t == 0 else [])) if k == 0 else [],
                        [(s_ten, 1)] if k == c.NK - 1 else [])
                vdone = s_ten.snap()

                def vev(e, t=t, sps=sps):
                    dst = AP(AR1, VB_COL + t * 260,
                             [[AR1W, 128], [65, 4], [1, 64]])
                    return e.tensor_copy(dst, sps.rearrange(
                        "p (h f) -> p h f", h=4))
                P.emit(V, vev, [vdone] + (ent if (sb == 0 and t == 0)
                                          else []), [(s_vec, 1)])
                slot_war[si] = [s_vec.snap()]
            st_wv.set_war([s_ten.snap()])

            def vones(e):
                dst = AP(AR1, VB_COL + 64,
                         [[AR1W, 128], [260, c.NT], [65, 4], [1, 1]])
                return e.memset(dst, 1.0)
            P.emit(V, vones, [s_vec.snap()], [(s_vec, 1)])
            vready[0] = s_vec.snap()

        def qkproj(p):
            for which, wT_d, dcol, kscale in (
                    ("q", wqT_d, QB_COL, 1.0), ("k", wkT_d, KB_COL, 0.125)):
                wslot = st_w2.k % 2
                wqk = a3(WQK_COL + wslot * 2048, 2048)
                ww = st_w2.load(G, wqk.rearrange("p (k f) -> p k f", f=128),
                                AP(wT_d.tensor, p * 128,
                                   [[c.DM, 128], [128 * c.DM, c.NK],
                                    [1, 128]]),
                                waits=(ent + [lnxT_w])
                                if (p == 0 and which == "q") else [])
                qsb = a1(QSB_COL, c.S)
                qsw = a1(QSW_COL, c.S)
                for ns in range(4):
                    si = ps1_slot()
                    sps = PS1[:, si * 512:(si + 1) * 512]
                    for k in range(c.NK):
                        P.emit(T, lambda e, k=k, ns=ns, sps=sps, wqk=wqk:
                               e.matmul(
                            sps, wqk[:, k * 128:(k + 1) * 128],
                            a2(k * c.S + ns * 512, 512),
                            start=(k == 0), stop=(k == c.NK - 1),
                            skip_group_check=True),
                            ([ww] + slot_war[si] +
                             ([lnxT_w] if (p == 0 and ns == 0) else []))
                            if k == 0 else [],
                            [(s_ten, 1)] if k == c.NK - 1 else [])
                    pdone = s_ten.snap()
                    P.emit(V, lambda e, ns=ns, sps=sps, qsb=qsb, ks=kscale:
                           e.tensor_scalar(
                        qsb[:, ns * 512:(ns + 1) * 512], sps, ks, None,
                        ALU.mult),
                        [pdone] + (qsb_war[0] if ns == 0 else []),
                        [(s_vec, 1)])
                    slot_war[si] = [s_vec.snap()]
                st_w2.set_war([s_ten.snap()])
                qsb_w = s_vec.snap()
                for (d0, sc0) in ((0, 32), (32, 0), (64, 96), (96, 64)):
                    P.dma(SY, qsw[d0:d0 + 32, :], qsb[sc0:sc0 + 32, :],
                          s_sydma, waits=[qsb_w] + qsw_war[0])
                dstb = a1(dcol, c.S)
                P.emit(V, lambda e, dstb=dstb, qsb=qsb: e.tensor_mul(
                    dstb, qsb, ropeC[:, 0:c.S]),
                    [boot_w, s_ten.snap(), s_vec.snap()] +
                    (ent if p == 0 else []), [(s_vec, 1)])
                P.emit(V, lambda e, qsb=qsb, qsw=qsw: e.tensor_mul(
                    qsb, qsw, ropeS[:, 0:c.S]),
                    [s_sydma.snap()], [(s_vec, 1)])
                qsw_war[0] = [s_vec.snap()]
                P.emit(V, lambda e, dstb=dstb, qsb=qsb: e.tensor_add(
                    dstb, dstb, qsb), [], [(s_vec, 1)])
                qsb_war[0] = [s_vec.snap()]
            qk_w[0] = s_vec.snap()

        def s_strip(h, J, eb_col, ebw, first_mm):
            hh2 = h % 2
            sl = (h * 16 + J) % NPSL
            pw = pslot_war[sl]
            fscl, fvec = [True], [True]
            for q in range(J // 4, 4):
                c0 = max(0, 512 * q - 128 * J)
                c1 = 512 * (q + 1) - 128 * J
                w = c1 - c0
                i0 = 128 * J + c0
                si = ps1_slot()
                sps = PS1[:, si * 512:si * 512 + w]
                P.emit(T, lambda e, J=J, i0=i0, w=w, sps=sps, hh2=hh2:
                       e.matmul(
                    sps, a1(KB_COL + J * 128, 128)[64 * hh2:64 * hh2 + 64, :],
                    a1(QB_COL + i0, w)[64 * hh2:64 * hh2 + 64, :],
                    start=True, stop=True, skip_group_check=True),
                    slot_war[si] + ([qk_w[0]] if first_mm[0] else []),
                    [(s_ten, 1)])
                first_mm[0] = False
                smm = s_ten.snap()
                rdy = []
                ib1 = min(c1, BAND)
                if ib1 > c0:
                    ei = exps_i[0] % 4
                    exps_i[0] += 1
                    es = a3(EXPS_COL + ei * 512, 512)
                    P.emit(SC, lambda e, es=es, sps=sps, n=ib1 - c0:
                           e.activation(es[:, 0:n], sps[:, 0:n], AF.Exp),
                           [smm] + exps_war[ei], [(s_scl, 1)])
                    sw = s_scl.snap()
                    P.emit(V, lambda e, sl=sl, c0=c0, n=ib1 - c0, es=es,
                           eb_col=eb_col: e.tensor_mul(
                        a1(PB_COL + sl * 2048 + c0, n), es[:, 0:n],
                        a3(eb_col + 384 + c0, n)),
                        [sw, ebw] + (pw if fvec[0] else []),
                        [(s_vec, 1)])
                    fvec[0] = False
                    exps_war[ei] = [s_vec.snap()]
                    rdy.append(s_vec.snap())
                if c1 > ib1:
                    o0 = max(c0, ib1)
                    P.emit(SC, lambda e, sl=sl, o0=o0, c0=c0, c1=c1, sps=sps:
                           e.activation(
                        a1(PB_COL + sl * 2048 + o0, c1 - o0),
                        sps[:, o0 - c0:c1 - c0], AF.Exp),
                        [smm] + (pw if fscl[0] else []),
                        [(s_scl, 1)])
                    fscl[0] = False
                    rdy.append(s_scl.snap())
                slot_war[si] = [s_scl.snap()]
                piece_ready[(h, J, q)] = rdy

        def pv_strip(h, J):
            hh4 = h % 4
            sl = (h * 16 + J) % NPSL
            for q in range(J // 4, 4):
                c0 = max(0, 512 * q - 128 * J)
                c1 = 512 * (q + 1) - 128 * J
                o0 = 128 * J + c0
                w = c1 - c0
                start = (J == 0)
                stop = (J == min(4 * q + 3, 15))
                ws = list(piece_ready.pop((h, J, q), []))
                if start:
                    ws += bp_war[q] + [vready[0]]
                P.emit(T, lambda e, hh4=hh4, sl=sl, c0=c0, o0=o0, w=w, J=J,
                       start=start, stop=stop: e.matmul(
                    PS2[0:65, o0:o0 + w],
                    AP(AR1, VB_COL + J * 260 + hh4 * 65,
                       [[AR1W, 128], [1, 65]]),
                    a1(PB_COL + sl * 2048 + c0, w),
                    start=start, stop=stop, skip_group_check=True),
                    ws, [(s_ten, 1)] if (stop or q == 3) else [])
                if stop:
                    bank_done[(h, q)] = s_ten.snap()
            pslot_war[sl] = [s_ten.snap()]

        def head_tail(h):
            ssl = h % 2
            stg = a3(STG_COL + ssl * 2048, 2048)[0:64, :]
            dws = []
            for g in (0, 1):
                bws = [bank_done[(h, 2 * g)], bank_done[(h, 2 * g + 1)]]
                P.emit(V, lambda e, g=g, stg=stg: e.tensor_copy(
                    stg[:, 1024 * g:1024 * (g + 1)],
                    PS2[0:64, 1024 * g:1024 * (g + 1)]),
                    bws + (stage_war[ssl] if g == 0 else []), [(s_vec, 1)])
                evac_done[(h, g)] = [s_vec.snap()]
                P.emit(V, lambda e, g=g, ssl=ssl, h=h: e.tensor_scalar(
                    dnm(ssl)[:, 1024 * g:1024 * (g + 1)],
                    PS2[64:65, 1024 * g:1024 * (g + 1)],
                    soff64[64:65, c.H + h:c.H + h + 1], None, ALU.add),
                    bws + [eoff_w] + (dnm_war[ssl] if g == 0 else []),
                    [(s_vec, 1)])
                evac_done[(h, g)].append(s_vec.snap())
                dws.append(s_vec.snap())
                bp_war[2 * g] = list(evac_done[(h, g)])
                bp_war[2 * g + 1] = list(evac_done[(h, g)])
            rt = rtile(ssl)
            P.dma(SY, rt, dnm(ssl), s_sydma, waits=dws)
            P.emit(V, lambda e, rt=rt: e.reciprocal(rt, rt),
                   [s_sydma.snap()], [(s_vec, 1)])
            P.dma(SY, rec(ssl), rt, s_sydma,
                  waits=[s_vec.snap()] + dnm_war[ssl])
            recip_done[h] = s_sydma.snap()

        def bc_mult_store(h):
            ssl = h % 2
            stg = a3(STG_COL + ssl * 2048, 2048)[0:64, :]
            mws = []
            for q in range(4):
                si = ps1_slot()
                bca = PS1[0:64, si * 512:si * 512 + 512]
                P.emit(T, lambda e, q=q, ssl=ssl, bca=bca: e.matmul(
                    bca, onesr[0:1, 0:64], rec(ssl, 512 * q, 512),
                    start=True, stop=True, skip_group_check=True),
                    slot_war[si] + ([recip_done[h]] if q == 0 else []),
                    [(s_ten, 1)])
                bcw = s_ten.snap()
                P.emit(V, lambda e, q=q, stg=stg, bca=bca: e.tensor_tensor(
                    stg[:, 512 * q:512 * (q + 1)],
                    stg[:, 512 * q:512 * (q + 1)], bca, ALU.mult),
                    [bcw], [(s_vec, 1)])
                slot_war[si] = [s_vec.snap()]
                mult_done[(h, q)] = s_vec.snap()
                mws.append(mult_done[(h, q)])
            dnm_war[ssl] = [s_ten.snap()]
            k, par = h // 2, h % 2
            dst = AP(attnT_d.tensor,
                     b * c.NT * 128 * c.NK * 128 + 64 * par * 2048 + k * 128,
                     [[2048, 64], [128 * c.NK * 128, c.NT], [1, 128]])
            stw = st_at.load(SC, dst,
                             stg.rearrange("p (t f) -> p t f", f=128),
                             waits=mws)
            stage_war[ssl] = [stw]

        for h in range(c.H):
            if h % 4 == 0:
                vproj(h // 4)
            if h % 2 == 0:
                qkproj(h // 2)
            ebslot = st_eb.k % 2
            eb_col = EB_COL + ebslot * c.C_EB
            ebw = st_eb.emit_load(
                G, lambda e, eb_col=eb_col, h=h: e.indirect_dma_start(
                    a3(eb_col, c.C_EB), None,
                    explb_d.rearrange("h n -> (h n)").unsqueeze(1),
                    IndirectOffsetOnAxis(dgidx[:, 0:1], 0),
                    element_offset=h * (c.PADN + c.S)),
                waits=[explb_w, boot_w] + (ent if h == 0 else []))
            if h == c.H - 1:
                lnfree = s_ten.snap()
                for pc in range(4):
                    P.dma(G, a2(pc * 4 * c.DM, 4 * c.DM).rearrange(
                        "p (k f) -> p k f", f=c.DM),
                        AP(woT_d.tensor, pc * 4 * 128 * c.DM,
                           [[c.DM, 128], [128 * c.DM, 4], [1, c.DM]]),
                        s_woL, waits=[lnfree])
                    woL_ws.append(s_woL.snap())
            first_mm = [True]
            for J in range(c.NK):
                s_strip(h, J, eb_col, ebw, first_mm)
                if J >= 3:
                    pv_strip(h, J - 3)
                if J == 6 and h >= 1:
                    bc_mult_store(h - 1)
            st_eb.set_war([s_vec.snap()])
            for J in range(c.NK - 3, c.NK):
                pv_strip(h, J)
            head_tail(h)
        bc_mult_store(c.H - 1)
        return st_at.total(), woL_ws

    def wo_phase(b, store_ws, piece_ws):
        zst_w = {0: [], 1: []}
        psum_war = {0: [], 1: []}
        wo_entry = [s_vec.snap(), s_scl.snap(), s_ten.snap()]
        for t in range(c.NT):
            sl = t % 2
            att = a1(sl * c.DM, c.DM)
            atw = st_x.load(SY, att, attnT_d[b][t],
                            waits=(list(store_ws) + wo_entry)
                            if t < 2 else [])
            xt = a1f(2048 + sl * c.DM, c.DM)
            xw = st_ln.load(SY, xt, x_d[b][t * 128:(t + 1) * 128, :],
                            waits=wo_entry if t < 2 else [])
            yps = (PS1 if sl == 0 else PS2)[:, 0:c.DM]
            for k in range(c.NK):
                for ns in range(4):
                    ws = []
                    if k == 0 and ns == 0:
                        ws = [atw] + psum_war[sl] + \
                            (wo_entry if t < 2 else [])
                    if t == 0 and ns == 0 and k % 4 == 0:
                        ws = ws + [piece_ws[k // 4]]
                    P.emit(T, lambda e, k=k, ns=ns, att=att, yps=yps:
                           e.matmul(
                        yps[:, ns * 512:(ns + 1) * 512],
                        att[:, k * 128:(k + 1) * 128],
                        a2(k * c.DM + ns * 512, 512),
                        start=(k == 0), stop=(k == c.NK - 1),
                        skip_group_check=True),
                        ws, [(s_ten, 1)] if (k == c.NK - 1 and ns == 3)
                        else [])
            z1t = a1f(DNM_F + sl * c.DM, c.DM)
            P.emit(V, lambda e, z1t=z1t, xt=xt, yps=yps:
                   e.scalar_tensor_tensor(
                z1t, yps, 1.0, xt, ALU.mult, ALU.add),
                [s_ten.snap(), xw] + zst_w[sl], [(s_vec, 1)])
            psum_war[sl] = [s_vec.snap()]
            st_x.set_war([s_ten.snap()])
            st_ln.set_war([s_vec.snap()])
            zst_w[sl] = [st_z.load(SC, z1_d[b][t * 128:(t + 1) * 128, :],
                                   z1t, waits=[s_vec.snap()])]
        return s_vec.snap()
    def shared_phase(b, ln2_w, entry):
        potst_w = {0: []}
        shT_col = 0
        assert c.NIS * c.S <= AR1W
        POT_B = AR3W - 2 * c.DM
        TMP_COL = POT_B
        assert TMP_COL + c.S <= AR3W and TMP_COL >= 4 * c.ISH
        for isc in range(c.NIS):
            wgb = a3((st_w.k % 2) * c.ISH, c.NK * 128)
            wgw = st_w.load(G, wgb.rearrange("p (k f) -> p k f", f=128),
                            AP(swg_d.tensor, isc * 128,
                               [[c.ISH, 128], [128 * c.ISH, c.NK], [1, 128]]),
                            waits=list(entry) if isc == 0 else [])
            wub = a3(2 * c.ISH + (st_w2.k % 2) * c.ISH, c.NK * 128)
            wuw = st_w2.load(G, wub.rearrange("p (k f) -> p k f", f=128),
                             AP(swu_d.tensor, isc * 128,
                                [[c.ISH, 128], [128 * c.ISH, c.NK],
                                 [1, 128]]),
                             waits=list(entry) if isc == 0 else [])
            gps = PS1[:, 0:min(c.S, 2048)]
            ups = PS2[:, 0:min(c.S, 2048)]
            nstr = cdiv(min(c.S, 2048), 512)
            for k in range(c.NK):
                for ns in range(nstr):
                    n0, n1 = ns * 512, min(c.S, (ns + 1) * 512)
                    P.emit(T, lambda e, k=k, n0=n0, n1=n1, wgb=wgb: e.matmul(
                        gps[:, n0:n1], wgb[:, k * 128:(k + 1) * 128],
                        a2(k * c.S + n0, n1 - n0),
                        start=(k == 0), stop=(k == c.NK - 1)),
                        [wgw, ln2_w, s_scl.snap()] if (k == 0 and ns == 0)
                        else [],
                        [(s_ten, 1)] if (k == c.NK - 1 and ns == nstr - 1)
                        else [])
            g_done = s_ten.snap()
            st_w.set_war(g_done)
            for k in range(c.NK):
                for ns in range(nstr):
                    n0, n1 = ns * 512, min(c.S, (ns + 1) * 512)
                    P.emit(T, lambda e, k=k, n0=n0, n1=n1, wub=wub: e.matmul(
                        ups[:, n0:n1], wub[:, k * 128:(k + 1) * 128],
                        a2(k * c.S + n0, n1 - n0),
                        start=(k == 0), stop=(k == c.NK - 1)),
                        [wuw, s_vec.snap()] if (k == 0 and ns == 0) else [],
                        [(s_ten, 1)] if (k == c.NK - 1 and ns == nstr - 1)
                        else [])
            u_done = s_ten.snap()
            st_w2.set_war(u_done)
            tmp = a3(TMP_COL, c.S)
            P.emit(SC, lambda e, tmp=tmp: e.activation(
                tmp, PS1[:, 0:c.S], AF.Silu),
                [g_done, s_vec.snap()], [(s_scl, 1)])
            P.emit(V, lambda e, tmp=tmp, isc=isc: e.tensor_tensor(
                a1(shT_col + isc * c.S, c.S), tmp, PS2[:, 0:c.S], ALU.mult),
                [u_done, s_scl.snap()] +
                (list(entry) if isc == 0 else []), [(s_vec, 1)])
        sh_done = s_vec.snap()
        n_ar2 = min(c.NIS, AR2W // c.DM)
        assert (c.NIS - n_ar2) * c.DM <= POT_B
        swd_cw = []
        for isc in range(c.NIS):
            dst = (a2(isc * c.DM, c.DM) if isc < n_ar2
                   else a3((isc - n_ar2) * c.DM, c.DM))
            w = st_swd.load(G, dst, AP(swd_d.tensor, isc * 128 * c.DM,
                                       [[c.DM, 128], [1, c.DM]]),
                            waits=[sh_done] if isc < 8 else [])
            st_swd.set_war(w)
            swd_cw.append(w)
        for t in range(c.NT):
            yps = PS1[:, 0:min(c.DM, 2048)]
            war = s_vec.snap()
            nstr = cdiv(min(c.DM, 2048), 512)
            for isc in range(c.NIS):
                rhs = (a2(isc * c.DM, c.DM) if isc < n_ar2
                       else a3((isc - n_ar2) * c.DM, c.DM))
                for ns in range(nstr):
                    n0, n1 = ns * 512, min(c.DM, (ns + 1) * 512)
                    P.emit(T, lambda e, isc=isc, n0=n0, n1=n1, t=t, rhs=rhs:
                           e.matmul(
                        yps[:, n0:n1],
                        a1(shT_col + isc * c.S + t * 128, 128),
                        rhs[:, n0:n1],
                        start=(isc == 0), stop=(isc == c.NIS - 1)),
                        ([sh_done, war] if (isc == 0 and ns == 0) else []) +
                        ([swd_cw[isc]] if (t == 0 and ns == 0) else []),
                        [(s_ten, 1)] if (isc == c.NIS - 1 and
                                         ns == nstr - 1) else [])
            pot = a3(TMP_COL + c.S, c.DM)
            P.emit(V, lambda e, pot=pot: e.tensor_copy(pot, yps),
                   [s_ten.snap()] + potst_w[0], [(s_vec, 1)])
            ng = b * c.NT + t
            potst_w[0] = [st_po.load(
                SC, preout_d[ng * 128:(ng + 1) * 128, :], pot,
                waits=[s_vec.snap()])]
        return s_vec.snap()

    def routing_phase(logits_w):
        l3 = ltok[:].rearrange("p (f e) -> p f e", e=c.E)
        p3 = probs[:].rearrange("p (f e) -> p f e", e=c.E)
        P.emit(V, lambda e: e.reduce_max(gmax[:].unsqueeze(2), l3,
                                         axis=AX.X),
               [logits_w, s_vec.snap()], [(s_vec, 1)])
        P.emit(V, lambda e: e.tensor_sub(
            p3, l3, gmax[:].unsqueeze(2).to_broadcast([128, c.NF, c.E])),
            [s_vec.snap()], [(s_vec, 1)])
        P.emit(SC, lambda e: e.activation(probs[:], probs[:], AF.Exp),
               [s_vec.snap()], [(s_scl, 1)])
        P.emit(V, lambda e: e.reduce_sum(gsum[:].unsqueeze(2), p3,
                                         axis=AX.X),
               [s_scl.snap()], [(s_vec, 1)])
        P.emit(V, lambda e: e.reciprocal(gsum[:], gsum[:]),
               [s_vec.snap()], [(s_vec, 1)])
        P.emit(V, lambda e: e.tensor_tensor(
            p3, p3, gsum[:].unsqueeze(2).to_broadcast([128, c.NF, c.E]),
            ALU.mult), [s_vec.snap()], [(s_vec, 1)])
        for f in range(c.NF):
            P.emit(V, lambda e, f=f: e.max(
                top8[:], probs[:, f * c.E:(f + 1) * c.E]),
                [s_vec.snap()], [(s_vec, 1)])
            P.emit(V, lambda e, f=f: e.max_index(
                idx8[:], top8[:], probs[:, f * c.E:(f + 1) * c.E]),
                [s_vec.snap()], [(s_vec, 1)])
            P.emit(V, lambda e, f=f: e.tensor_copy(
                w12[:].rearrange("p (c f) -> p c f", f=c.NF)[:, :, f],
                top8[:, 0:2]), [s_vec.snap()], [(s_vec, 1)])
            P.emit(V, lambda e, f=f: e.tensor_copy(
                i12f[:].rearrange("p (c f) -> p c f", f=c.NF)[:, :, f],
                idx8[:, 0:2]), [s_vec.snap()], [(s_vec, 1)])
        P.emit(V, lambda e: e.memset(pos12[:], 0.0), [s_vec.snap()],
               [(s_vec, 1)])
        pps = PS1[:, 0:c.NF]
        cps = PS1[0:1, 512:512 + c.NF]
        ctp = PS1[0:c.NF, 1024:1025]
        crp = PS1[0:1, 1536:1536 + c.NF]
        i1v = i12f[:].rearrange("p (c f) -> p c f", f=c.NF)
        p12v = pos12[:].rearrange("p (c f) -> p c f", f=c.NF)
        for ee in range(c.E):
            P.emit(V, lambda e, ee=ee: e.tensor_single_scalar(
                msk1[:], i1v[:, 0, :], float(ee), ALU.is_equal),
                [s_vec.snap()], [(s_vec, 1)])
            P.emit(V, lambda e, ee=ee: e.tensor_single_scalar(
                mske[:], i1v[:, 1, :], float(ee), ALU.is_equal),
                [s_vec.snap()], [(s_vec, 1)])
            P.emit(V, lambda e: e.tensor_add(mske[:], mske[:], msk1[:]),
                   [s_vec.snap()], [(s_vec, 1)])
            mw = s_vec.snap()
            P.emit(T, lambda e: e.matmul(pps, triu[:], mske[:],
                                         start=True, stop=False,
                                         skip_group_check=True),
                   [mw, boot_w, s_vec.snap()], [(s_ten, 1)])
            P.emit(T, lambda e: e.matmul(cps, onesc[:], mske[:],
                                         start=True, stop=True,
                                         skip_group_check=True),
                   [], [(s_ten, 1)])
            P.emit(V, lambda e: e.tensor_copy(csum[:], cps),
                   [s_ten.snap()], [(s_vec, 1)])
            P.emit(T, lambda e: e.transpose(ctp, csum[:], identf[0:1, 0:1]),
                   [s_vec.snap()], [(s_ten, 1)])
            P.emit(V, lambda e: e.tensor_copy(csumT[0:c.NF, :], ctp),
                   [s_ten.snap()], [(s_vec, 1)])
            P.emit(T, lambda e: e.matmul(crp, csumT[0:c.NF, :],
                                         triu[0:c.NF, 0:c.NF],
                                         start=True, stop=True,
                                         skip_group_check=True),
                   [s_vec.snap()], [(s_ten, 1)])
            P.emit(V, lambda e: e.tensor_copy(carrys[:], crp),
                   [s_ten.snap()], [(s_vec, 1)])
            P.emit(T, lambda e: e.matmul(pps, onesr[:], carrys[:],
                                         start=False, stop=True,
                                         skip_group_check=True),
                   [s_vec.snap()], [(s_ten, 1)])
            P.emit(V, lambda e: e.tensor_copy(pose[:], pps),
                   [s_ten.snap()], [(s_vec, 1)])
            for cc in range(2):
                P.emit(V, lambda e, ee=ee, cc=cc: e.tensor_single_scalar(
                    msk1[:], i1v[:, cc, :], float(ee), ALU.is_equal),
                    [s_vec.snap()], [(s_vec, 1)])
                P.emit(V, lambda e: e.tensor_mul(msk1[:], msk1[:], pose[:]),
                       [s_vec.snap()], [(s_vec, 1)])
                P.emit(V, lambda e, cc=cc: e.tensor_add(
                    p12v[:, cc, :], p12v[:, cc, :], msk1[:]),
                    [s_vec.snap()], [(s_vec, 1)])
        P.emit(V, lambda e: e.tensor_single_scalar(
            pos12[:], pos12[:], float(c.CAP - 1), ALU.min),
            [s_vec.snap()], [(s_vec, 1)])
        P.emit(V, lambda e: e.scalar_tensor_tensor(
            scatf[:], i12f[:], float(c.CAP), pos12[:], ALU.mult, ALU.add),
            [s_vec.snap()], [(s_vec, 1)])
        P.emit(V, lambda e: e.tensor_copy(scat[:], scatf[:]),
               [s_vec.snap()], [(s_vec, 1)])
        return s_vec.snap()

    def inv_phase(route_w, entry):
        scv = scat[:].rearrange("p (c f) -> p c f", f=c.NF)
        war_v = {0: [], 1: []}
        for f in range(c.NF):
            sl = f % 2
            for cc in range(2):
                b0 = (sl * 2 + cc) * 4
                P.emit(V, lambda e, f=f, b0=b0: e.tensor_scalar(
                    invi[:, b0:b0 + 1], iotas[:, 0:1], float(f * 128), None,
                    ALU.add), [boot_w, route_w] + war_v[sl] +
                    (list(entry) if f < 2 else []), [(s_vec, 1)])
                P.emit(V, lambda e, cc=cc, b0=b0: e.tensor_scalar(
                    invi[:, b0 + 1:b0 + 2], invi[:, b0:b0 + 1],
                    float(cc * c.NTOK), None, ALU.add),
                    [s_vec.snap()], [(s_vec, 1)])
                P.emit(V, lambda e, b0=b0: e.tensor_copy(
                    inviu[:, b0:b0 + 2], invi[:, b0:b0 + 2]),
                    [s_vec.snap()], [(s_vec, 1)])
                P.emit(V, lambda e, f=f, cc=cc, b0=b0: e.tensor_copy(
                    inviu[:, b0 + 2:b0 + 3],
                    w12[:, cc * c.NF + f:cc * c.NF + f + 1].bitcast(U32)),
                    [s_vec.snap()], [(s_vec, 1)])
                vw = s_vec.snap()
                st_sc.emit_load(G, lambda e, f=f, cc=cc, b0=b0:
                                e.indirect_dma_start(
                    inv_d, IndirectOffsetOnAxis(scv[:, cc, f:f + 1], 0),
                    inviu[:, b0:b0 + 4], None), waits=[vw, route_w])
            war_v[sl] = st_sc.total()
        return st_sc.total()

    def expert_phase(inv_ws, ln2d_ws, entry):
        eost_w = {0: [], 1: []}
        sv_war = {tt: [] for tt in range(c.NCAP)}
        gv_ld = {}
        XG_COL = c.NI * c.CAP
        EOS_COL = XG_COL + 2 * c.DM
        WD1_COL = EOS_COL + 2 * c.DM
        n_wd1 = min(c.NI, (AR1W - WD1_COL) // c.DM)
        assert 4 * c.DM + c.CAP <= AR3W
        assert c.NK * c.CAP <= AR2W
        assert (c.NI - n_wd1) * c.DM <= AR2W - c.NK * c.CAP

        def wd_ap(isc):
            if isc < n_wd1:
                return a1(WD1_COL + isc * c.DM, c.DM)
            return a2(c.NK * c.CAP + (isc - n_wd1) * c.DM, c.DM)
        prev_down_w = []
        for ee in range(c.E):
            for isc in range(c.NI):
                P.dma(G, wd_ap(isc),
                      AP(wd_d.tensor, ee * c.DI * c.DM + isc * 128 * c.DM,
                         [[c.DM, 128], [1, c.DM]]),
                      s_wdL,
                      waits=(prev_down_w + (list(entry) if ee == 0 else []))
                      if isc == 0 else [])
            wd_w = s_wdL.snap()
            for tt in range(c.NCAP):
                sl = st_xg.k % 2
                xg = a1(XG_COL + sl * c.DM, c.DM)
                r0 = ee * c.CAP + tt * 128
                P.dma(SY, gv[:, 4 * tt:4 * tt + 4], inv_d[r0:r0 + 128, :],
                      s_sydma, waits=sv_war[tt] +
                      ((list(inv_ws) + list(entry))
                       if (ee == 0 and tt == 0) else []))
                gv_ld[tt] = s_sydma.snap()
                xw = st_xg.emit_load(
                    G, lambda e, xg=xg, tt=tt: e.indirect_dma_start(
                        xg, None, ln2t_d,
                        IndirectOffsetOnAxis(gv[:, 4 * tt:4 * tt + 1], 0)),
                    waits=[gv_ld[tt]] + ((list(ln2d_ws) + list(entry))
                                         if (ee == 0 and tt == 0) else []))
                evw = transpose_tile(
                    lambda k, xg=xg: xg[:, k * 128:(k + 1) * 128],
                    AR2, 0, c.CAP, tt * 128, c.NK, [xw] + (
                        list(entry) if (ee == 0 and tt == 0) else []))
                st_xg.set_war(s_ten.snap())
            xgT_w = s_vec.snap()
            for isc in range(c.NI):
                wgb = a3((st_w.k % 2) * c.DM, c.DM)
                wgw = st_w.load(G, wgb.rearrange("p (k f) -> p k f", f=128),
                                AP(wg_d.tensor, ee * c.DM * c.DI + isc * 128,
                                   [[c.DI, 128], [128 * c.DI, c.NK],
                                    [1, 128]]),
                                waits=list(entry) if (ee == 0 and isc == 0)
                                else [])
                wub = a3(2 * c.DM + (st_w2.k % 2) * c.DM, c.DM)
                wuw = st_w2.load(G, wub.rearrange("p (k f) -> p k f", f=128),
                                 AP(wu_d.tensor, ee * c.DM * c.DI +
                                    isc * 128,
                                    [[c.DI, 128], [128 * c.DI, c.NK],
                                     [1, 128]]),
                                 waits=list(entry) if (ee == 0 and isc == 0)
                                 else [])
                gps = PS1[:, 0:min(c.CAP, 2048)]
                ups = PS2[:, 0:min(c.CAP, 1536)]
                assert c.CAP <= 1536
                nstr = cdiv(c.CAP, 512)
                for k in range(c.NK):
                    for ns in range(nstr):
                        n0, n1 = ns * 512, min(c.CAP, (ns + 1) * 512)
                        P.emit(T, lambda e, k=k, n0=n0, n1=n1, wgb=wgb:
                               e.matmul(
                            gps[:, n0:n1], wgb[:, k * 128:(k + 1) * 128],
                            a2(k * c.CAP + n0, n1 - n0),
                            start=(k == 0), stop=(k == c.NK - 1)),
                            [wgw, xgT_w, s_scl.snap()]
                            if (k == 0 and ns == 0) else [],
                            [(s_ten, 1)] if (k == c.NK - 1 and
                                             ns == nstr - 1) else [])
                g_done = s_ten.snap()
                st_w.set_war(g_done)
                for k in range(c.NK):
                    for ns in range(nstr):
                        n0, n1 = ns * 512, min(c.CAP, (ns + 1) * 512)
                        P.emit(T, lambda e, k=k, n0=n0, n1=n1, wub=wub:
                               e.matmul(
                            ups[:, n0:n1], wub[:, k * 128:(k + 1) * 128],
                            a2(k * c.CAP + n0, n1 - n0),
                            start=(k == 0), stop=(k == c.NK - 1)),
                            [wuw, s_vec.snap()] if (k == 0 and ns == 0)
                            else [],
                            [(s_ten, 1)] if (k == c.NK - 1 and
                                             ns == nstr - 1) else [])
                u_done = s_ten.snap()
                st_w2.set_war(u_done)
                tmp = a3(4 * c.DM, c.CAP)
                P.emit(SC, lambda e, tmp=tmp: e.activation(
                    tmp, PS1[:, 0:c.CAP], AF.Silu),
                    [g_done, s_vec.snap()], [(s_scl, 1)])
                P.emit(V, lambda e, tmp=tmp, isc=isc: e.tensor_tensor(
                    a1(isc * c.CAP, c.CAP), tmp, PS2[:, 0:c.CAP], ALU.mult),
                    [u_done, s_scl.snap()] +
                    (list(entry) if (ee == 0 and isc == 0) else []),
                    [(s_vec, 1)])
            h_done = s_vec.snap()
            for tt in range(c.NCAP):
                yps = PS1[:, 0:min(c.DM, 2048)]
                war = s_vec.snap()
                nstr = cdiv(min(c.DM, 2048), 512)
                for isc in range(c.NI):
                    for ns in range(nstr):
                        n0, n1 = ns * 512, min(c.DM, (ns + 1) * 512)
                        P.emit(T, lambda e, isc=isc, n0=n0, n1=n1, tt=tt:
                               e.matmul(
                            yps[:, n0:n1],
                            a1(isc * c.CAP + tt * 128, 128),
                            wd_ap(isc)[:, n0:n1],
                            start=(isc == 0), stop=(isc == c.NI - 1)),
                            [wd_w, h_done, war] if (isc == 0 and ns == 0)
                            else [],
                            [(s_ten, 1)] if (isc == c.NI - 1 and
                                             ns == nstr - 1) else [])
                sl = st_eo.k % 2
                eos = a1(EOS_COL + sl * c.DM, c.DM)
                P.emit(V, lambda e, eos=eos, tt=tt: e.tensor_scalar(
                    eos, yps, gv[:, 4 * tt + 2:4 * tt + 3].bitcast(F32),
                    None, ALU.mult),
                    [s_ten.snap(), gv_ld[tt]] + eost_w[sl], [(s_vec, 1)])
                scw = st_eo.emit_load(
                    G, lambda e, eos=eos, tt=tt: e.indirect_dma_start(
                        contrib_d,
                        IndirectOffsetOnAxis(gv[:, 4 * tt + 1:4 * tt + 2],
                                             0),
                        eos, None),
                    waits=[s_vec.snap()] +
                    ([expin_zero_w] if (ee == 0 and tt == 0) else []))
                eost_w[sl] = [scw]
                sv_war[tt] = [scw]
            prev_down_w = [s_ten.snap()]
        return st_eo.total()

    def final_phase(co_ws, preout_ws, z1_ws, entry):
        otst_w = {0: [], 1: []}
        for f in range(c.NF):
            b, t = f // c.NT, f % c.NT
            pot = a3(8192 + (f % 2) * c.DM, c.DM)
            pw = st_x.load(SC, pot, preout_d[f * 128:(f + 1) * 128, :],
                           waits=(list(preout_ws) + list(co_ws) +
                                  list(entry))
                           if f == 0 else [])
            z1t = a3f((f % 2) * c.DM, c.DM)
            zw = st_ln.load(SY, z1t, z1_d[b][t * 128:(t + 1) * 128, :],
                            waits=(list(z1_ws) + list(co_ws))
                            if f == 0 else [])
            g1 = a1((st_ga.k % 2) * c.DM, c.DM)
            gw1 = st_ga.load(G, g1, contrib_d[f * 128:(f + 1) * 128, :],
                             waits=(list(co_ws) + list(entry))
                             if f == 0 else [])
            g2 = a1(2 * c.DM + (st_gb.k % 2) * c.DM, c.DM)
            gw2 = st_gb.load(G, g2,
                             contrib_d[c.NTOK + f * 128:
                                       c.NTOK + (f + 1) * 128, :],
                             waits=list(co_ws) if f == 0 else [])
            ot = a2f((f % 2) * c.DM, c.DM)
            P.emit(V, lambda e, z1t=z1t, pot=pot, ot=ot: e.tensor_tensor(
                ot, z1t, pot, ALU.add),
                [zw, pw, s_vec.snap()] + otst_w[f % 2] +
                (list(entry) if f < 2 else []),
                [(s_vec, 1)])
            st_x.set_war(s_vec.snap())
            st_ln.set_war(s_vec.snap())
            P.emit(V, lambda e, g1=g1, ot=ot: e.tensor_add(ot, ot, g1),
                   [gw1, s_vec.snap()], [(s_vec, 1)])
            st_ga.set_war(s_vec.snap())
            P.emit(V, lambda e, g2=g2, ot=ot: e.tensor_add(ot, ot, g2),
                   [gw2, s_vec.snap()], [(s_vec, 1)])
            st_gb.set_war(s_vec.snap())
            otst_w[f % 2] = [st_out.load(
                SC, out_d[b][t * 128:(t + 1) * 128, :], ot,
                waits=[s_vec.snap()])]
        P.emit(SY, lambda e: None, st_out.total())

    z1b, atot = [], []
    for b in range(c.NB):
        lnw = ln_phase(b, "ln1", [] if b == 0 else list(atot[0]))
        store_ws, piece_ws = attn_batch(
            b, lnw, P.barrier() + st_z.total() +
            ([explb_w, expin_zero_w] if b == 0 else []))
        wo_phase(b, store_ws, piece_ws)
        z1b.append(st_z.total())
        atot.append(list(store_ws))
    z1_ws = st_z.total()
    ln2w0 = ln_phase(0, "ln2", z1b[0] + atot[1])
    shared_phase(0, ln2w0, P.barrier() + st_lnst.total() + st_z.total())
    ln2w1 = ln_phase(1, "ln2", z1b[1] + st_po.total())
    shared_phase(1, ln2w1, P.barrier() + st_lnst.total())
    preout_ws = st_po.total()
    logits_w = s_vec.snap()
    ln2d_ws = st_lnst.total()
    route_w = routing_phase(logits_w)
    inv_ws = inv_phase(route_w, P.barrier())
    co_ws = expert_phase(inv_ws, ln2d_ws, P.barrier())
    final_phase(co_ws, preout_ws, z1_ws, P.barrier())

    with nc.Block() as block:
        P.finalize(block)
    return nc


def host_constants(c: Cfg):
    import ml_dtypes
    half = 32
    inv = 1.0 / (10000.0 ** (np.arange(half, dtype=np.float64) / half))
    t = np.arange(c.S, dtype=np.float64)
    ang = np.outer(inv, t)
    cos, sin = np.cos(ang), np.sin(ang)
    C = np.tile(cos, (4, 1)).astype(np.float32)
    Srow = np.concatenate([-sin, sin, -sin, sin], 0).astype(np.float32)
    return {
        "cst": np.full((128, 1), 1e-5, np.float32),
        "identb": np.eye(128, dtype=np.float32).astype(ml_dtypes.bfloat16),
        "identf": np.eye(128, dtype=np.float32),
        "ropeC": np.ascontiguousarray(C),
        "ropeS": np.ascontiguousarray(Srow),
        "triu": np.triu(np.ones((128, 128), np.float32), 1),
        "onesr": np.ones((1, 128), np.float32),
        "onesc": np.ones((128, 1), np.float32),
        "dgidx": (128 - np.arange(128, dtype=np.uint32)).reshape(128, 1),
        "iota": np.arange(128, dtype=np.float32).reshape(128, 1),
    }


def host_inputs(c: Cfg, core, inputs):
    cc = np.ascontiguousarray
    x = inputs["L0_attn_hidden_states"]
    m = {
        "x": cc(x[core * c.NB:(core + 1) * c.NB]),
        "wqT": cc(np.asarray(inputs["L0_attn_Wq"]).T),
        "wkT": cc(np.asarray(inputs["L0_attn_Wk"]).T),
        "wvT": cc(np.asarray(inputs["L0_attn_Wv"]).T),
        "woT": cc(np.asarray(inputs["L0_attn_Wo"]).T),
        "lb": cc(np.asarray(inputs["L0_attn_learnable_bias"])),
        "soff": cc(np.asarray(inputs["L0_attn_softmax_offset"])
                   .reshape(1, c.H)),
        "gwT": cc(np.asarray(inputs["L0_ffn_gate_w"]).T),
        "wg": cc(np.asarray(inputs["L0_ffn_w_gate"])),
        "wu": cc(np.asarray(inputs["L0_ffn_w_up"])),
        "wd": cc(np.asarray(inputs["L0_ffn_w_down"])),
        "swg": cc(np.asarray(inputs["L0_ffn_sw_gate"])),
        "swu": cc(np.asarray(inputs["L0_ffn_sw_up"])),
        "swd": cc(np.asarray(inputs["L0_ffn_sw_down"])),
    }
    m.update(host_constants(c))
    return m




_CFG = Cfg()
_N_CORES = 8


def kernel(**inputs):
    import contextlib
    from concourse.bass_utils import run_bass_kernel_spmd
    c = _CFG
    nc = bass.Bass("TRN2", target_bir_lowering=False)
    ctx = contextlib.ExitStack()
    build_program(c, nc, ctx)
    ctx.close()
    nc.finalize()
    in_maps = [host_inputs(c, core, inputs) for core in range(_N_CORES)]
    res = run_bass_kernel_spmd(nc, in_maps, core_ids=list(range(_N_CORES)))
    outs = [np.asarray(r["out"]).reshape(c.NB, c.S, c.DM).astype(np.float32)
            for r in res.results]
    return np.concatenate(outs, axis=0)
